# version 20
# baseline (speedup 1.0000x reference)
import sys
from contextlib import ExitStack

import numpy as np
import ml_dtypes

sys.path.insert(0, "/opt/trn_rl_repo")
import concourse.bass as bass
import concourse.mybir as mybir
from concourse.bass_utils import run_bass_kernel_spmd
from concourse.alu_op_type import AluOpType as OP

F32 = mybir.dt.float32
BF16 = mybir.dt.bfloat16
I32 = mybir.dt.int32
AF = mybir.ActivationFunctionType
AX = mybir.AxisListType

D = 2048
S = 2048
NDC = D // 128
TSL = 1024
NSL = S // TSL
HID = 5632
NHC = HID // 128
NH = 16
NKV = 4
HD = 128
WIN = 512
SCALE_A = 128.0 ** -0.5
LN_SCALE_A = float(np.log(SCALE_A))

ENGINES = ("sync", "gpsimd", "tensor", "vector", "scalar")

DBG = {"enable": False}


class Em:

    def __init__(self, nc):
        self.nc = nc
        self.ops = {e: [] for e in ENGINES}
        self.sems = {}
        self.cnt = {}
        self.waited = {e: {} for e in ENGINES}

    def add_sem(self, name, handle):
        self.sems[name] = handle
        self.cnt[name] = 0

    def instr(self, eng, method, *args, waits=(), inc=None, incn=1, lowp=False, **kw):
        ws = []
        for wv in waits:
            if wv is None:
                continue
            s, v = wv
            if v is None or v <= 0:
                continue
            if self.waited[eng].get(s, 0) >= v:
                continue
            ws.append((s, v))
            self.waited[eng][s] = v
        marker = None
        if inc is not None:
            self.cnt[inc] += incn
            marker = (inc, self.cnt[inc])
        sems = self.sems

        nc_ = self.nc

        def fn(e, ws=tuple(ws), method=method, args=args, kw=kw, inc=inc,
               incn=incn, lowp=lowp):
            for s, v in ws:
                e.wait_ge(sems[s], v)
            if lowp:
                with nc_.allow_low_precision(reason="bf16 softmax-recip operand"):
                    ins = getattr(e, method)(*args, **kw)
            else:
                ins = getattr(e, method)(*args, **kw)
            if inc is not None:
                ins.then_inc(sems[inc], incn)

        self.ops[eng].append(fn)
        return marker

    def wait(self, eng, *waits):
        ws = []
        for wv in waits:
            if wv is None:
                continue
            s, v = wv
            if v is None or v <= 0:
                continue
            if self.waited[eng].get(s, 0) >= v:
                continue
            ws.append((s, v))
            self.waited[eng][s] = v
        if not ws:
            return
        sems = self.sems

        def fn(e, ws=tuple(ws)):
            for s, v in ws:
                e.wait_ge(sems[s], v)

        self.ops[eng].append(fn)

    def last(self, sem):
        return (sem, self.cnt[sem])


class PsumPool:

    def __init__(self, tensor_handle, nbanks):
        self.t = tensor_handle
        self.nbanks = nbanks
        self.readers = [[] for _ in range(nbanks)]
        self.ptr = 0

    def alloc(self, nb):
        if self.ptr + nb > self.nbanks:
            self.ptr = 0
        start = self.ptr
        self.ptr += nb
        waits = []
        for b in range(start, start + nb):
            waits.extend(self.readers[b])
            self.readers[b] = []
        ap = self.t[:, start * 512 : (start + nb) * 512]
        return ap, waits, (start, nb)

    def add_reader(self, slot, marker):
        start, nb = slot
        for b in range(start, start + nb):
            self.readers[b].append(marker)


def build_program(n_phases=4):
    nc = bass.Bass("TRN2", target_bir_lowering=False)

    xT = nc.dram_tensor("xT", [NDC, 128, S], F32, kind="ExternalInput").ap()
    wqa = nc.dram_tensor("wqa", [NDC, NDC, 128, 128], F32, kind="ExternalInput").ap()
    wka = nc.dram_tensor("wka", [NDC, NDC, 128, 128], F32, kind="ExternalInput").ap()
    wva = nc.dram_tensor("wva", [NDC, NDC, 128, 128], F32, kind="ExternalInput").ap()
    woa = nc.dram_tensor("woa", [NDC, NDC, 128, 128], F32, kind="ExternalInput").ap()
    biases = nc.dram_tensor("biases", [128, 64], F32, kind="ExternalInput").ap()
    U8 = mybir.dt.uint8
    wg = nc.dram_tensor("wg", [NHC, NDC, 128, 128], U8, kind="ExternalInput").ap()
    wu = nc.dram_tensor("wu", [NHC, NDC, 128, 128], U8, kind="ExternalInput").ap()
    wd = nc.dram_tensor("wd", [NDC, NHC, 128, 128], U8, kind="ExternalInput").ap()
    sgu = nc.dram_tensor("sgu", [128, 2 * NHC], F32, kind="ExternalInput").ap()
    sdn = nc.dram_tensor("sdn", [128, NDC], F32, kind="ExternalInput").ap()
    wqb = nc.dram_tensor("wqb", [NH, NDC, 128, 128], F32, kind="ExternalInput").ap()
    wkb = nc.dram_tensor("wkb", [NKV, NDC, 128, 128], F32, kind="ExternalInput").ap()
    wvb = nc.dram_tensor("wvb", [NKV, NDC, 128, 128], F32, kind="ExternalInput").ap()
    wob = nc.dram_tensor("wob", [NDC, NH, 128, 128], F32, kind="ExternalInput").ap()
    lnw = nc.dram_tensor("lnw", [128, NDC], F32, kind="ExternalInput").ap()
    ones_c = nc.dram_tensor("ones_c", [128, 128], BF16, kind="ExternalInput").ap()
    ident_c = nc.dram_tensor("ident_c", [128, 128], BF16, kind="ExternalInput").ap()
    neg128_c = nc.dram_tensor("neg128_c", [1, 128], BF16, kind="ExternalInput").ap()
    maskd_c = nc.dram_tensor("maskd_c", [128, 128], BF16, kind="ExternalInput").ap()
    maskw_c = nc.dram_tensor("maskw_c", [128, 128], BF16, kind="ExternalInput").ap()
    ident32_c = nc.dram_tensor("ident32_c", [128, 128], F32, kind="ExternalInput").ap()
    cos2_c = nc.dram_tensor("cos2_c", [128, S], BF16, kind="ExternalInput").ap()
    sin2_c = nc.dram_tensor("sin2_c", [128, S], BF16, kind="ExternalInput").ap()
    y = nc.dram_tensor("y", [NDC, 128, S], F32, kind="ExternalOutput").ap()
    if DBG["enable"]:
        ydbg = nc.dram_tensor("ydbg", [3, 8, 128, 640], F32, kind="ExternalOutput").ap()
        aodbg = nc.dram_tensor("aodbg", [128, NH * TSL], F32, kind="ExternalOutput").ap()

    st = ExitStack()
    E = st.enter_context
    em = Em(nc)
    for sname in ("dX", "dC", "oA", "oB", "pe", "dv", "ac", "gp",
                  "w0", "w1", "w2", "w3", "w4", "w5", "dD"):
        em.add_sem(sname, E(nc.semaphore(sname)))

    x_sb = E(nc.sbuf_tensor("x_sb", [128, NDC * S], BF16))
    ones_sb = E(nc.sbuf_tensor("ones_sb", [128, 128], BF16))
    ident_sb = E(nc.sbuf_tensor("ident_sb", [128, 128], BF16))
    neg128_sb = E(nc.sbuf_tensor("neg128_sb", [1, 128], BF16))
    b_sb = E(nc.sbuf_tensor("b_sb", [128, 64], F32))
    bqs_sb = E(nc.sbuf_tensor("bqs_sb", [128, 16], F32))
    sgu_sb = E(nc.sbuf_tensor("sgu_sb", [128, 2 * NHC], F32))
    sdn_sb = E(nc.sbuf_tensor("sdn_sb", [128, NDC], F32))
    lnw_sb = E(nc.sbuf_tensor("lnw_sb", [128, NDC], F32))
    mst_sb = E(nc.sbuf_tensor("mst_sb", [128, TSL], BF16))
    cst_sb = E(nc.sbuf_tensor("cst_sb", [128, TSL], BF16))

    def xc(dc, sl=None):
        if sl is None:
            return x_sb[:, dc * S : (dc + 1) * S]
        return x_sb[:, dc * S + sl * TSL : dc * S + sl * TSL + TSL]

    for dc in range(NDC):
        em.instr("gpsimd", "dma_start", x_sb[:, dc * S : (dc + 1) * S], xT[dc, :, :],
                 inc="dX", incn=16)
    x_loaded = em.last("dX")
    for dst, src in ((ones_sb, ones_c), (ident_sb, ident_c), (neg128_sb, neg128_c),
                     (b_sb, biases), (sgu_sb, sgu), (sdn_sb, sdn), (lnw_sb, lnw)):
        em.instr("sync", "dma_start", dst[:], src, inc="dC", incn=16)
    consts_loaded = em.last("dC")
    em.instr("vector", "tensor_scalar_mul", bqs_sb[:, :], b_sb[:, 0:16], SCALE_A,
             waits=[consts_loaded], inc="dv")
    constv_sb = E(nc.sbuf_tensor("constv_sb", [128, 4], F32))
    for j, v in enumerate((1e-5, 1e-11, LN_SCALE_A, -25.0)):
        em.instr("vector", "memset", constv_sb[:, j : j + 1], v, inc="dv")
        nc.const_aps.register(F32, v, constv_sb[:, j : j + 1])
    consts_ready = em.last("dv")
    em.wait("tensor", consts_loaded)
    em.wait("scalar", consts_ready)

    def ln_stats(sl, pool, kind, xwaits, xsq_sb, t_b, war_mark):
        ps_sum, w1, slot1 = pool.alloc(2)
        ps_sq, w2, slot2 = pool.alloc(2)
        sq_mk = [None] * NDC
        for dc in range(NDC):
            pp = xsq_sb[:, (dc % 2) * TSL : (dc % 2 + 1) * TSL]
            war = sq_mk[dc - 2] if dc >= 2 else None
            em.instr("vector", "tensor_mul", pp, xc(dc, sl), xc(dc, sl),
                     waits=list(xwaits) + [war], inc="dv")
            sq_ready = em.last("dv")
            for nh in range(2):
                cols = slice(nh * 512, nh * 512 + 512)
                em.instr("tensor", "matmul", ps_sum[:, cols], ones_sb[:, :],
                         xc(dc, sl)[:, cols], start=(dc == 0), stop=(dc == NDC - 1),
                         waits=list(xwaits) + (w1 if dc == 0 else []))
                sq_mk[dc] = em.instr("tensor", "matmul", ps_sq[:, cols], ones_sb[:, :],
                                     pp[:, cols], start=(dc == 0), stop=(dc == NDC - 1),
                                     waits=[sq_ready] + (w2 if dc == 0 else []),
                                     inc="pe")
        sums_done = em.last("pe")
        em.instr("vector", "tensor_scalar_mul", mst_sb[:], ps_sum[:, 0:TSL], 1.0 / D,
                 waits=[sums_done, war_mark], inc="dv")
        em.instr("vector", "tensor_scalar_mul", t_b[:], ps_sq[:, 0:TSL], 1.0 / D, inc="dv")
        em.instr("vector", "tensor_mul", cst_sb[:], mst_sb[:], mst_sb[:], inc="dv")
        em.instr("vector", "tensor_sub", t_b[:], t_b[:], cst_sb[:], inc="dv")
        var_done = em.last("dv")
        pool.add_reader(slot1, var_done)
        pool.add_reader(slot2, var_done)
        if kind == "attnb":
            a_, b_ = 1.000001, 1e-11
        else:
            a_, b_ = 1.0, 1e-5
        em.instr("scalar", "activation", t_b[:], t_b[:], AF.Ln, scale=a_, bias=b_,
                 waits=[var_done], inc="ac")
        em.instr("scalar", "activation", cst_sb[:], t_b[:], AF.Exp, scale=-0.5, inc="ac")
        return em.last("ac")

    def ln_apply(sl, xn_sb, kind, c_done, u_b):
        for dc in range(NDC):
            xn = xn_sb[:, dc * TSL : (dc + 1) * TSL]
            em.instr("vector", "tensor_sub", u_b[:], xc(dc, sl), mst_sb[:],
                     waits=[c_done], inc="dv")
            if kind == "attnb":
                em.instr("vector", "tensor_mul", u_b[:], u_b[:], cst_sb[:], inc="dv")
                em.instr("vector", "tensor_scalar_mul", xn, u_b[:],
                         lnw_sb[:, dc : dc + 1], inc="dv")
            else:
                em.instr("vector", "tensor_mul", xn, u_b[:], cst_sb[:], inc="dv")
        return em.last("dv")

    class WStream:

        def __init__(self, buf, nslots, blkcols, sems):
            self.buf = buf
            self.nslots = nslots
            self.blkcols = blkcols
            self.readers = [None] * nslots
            self.sems = sems
            self.i = 0

        def slot_ap(self, i):
            c = (i % self.nslots) * self.blkcols
            return self.buf[:, c : c + self.blkcols]

        def load(self, src_tiled, blk, nchunks):
            return self.load_ap(src_tiled[blk], nchunks)

        def load_ap(self, ap3d, nchunks):
            i = self.i
            self.i += 1
            sem = self.sems[i % self.nslots]
            em.instr("gpsimd", "dma_start",
                     self.slot_ap(i).rearrange("p (c q) -> p c q", c=nchunks),
                     ap3d.rearrange("c p q -> p c q"),
                     waits=[self.readers[i % self.nslots]], inc=sem, incn=16)
            return i, em.last(sem)

        def set_reader(self, i, marker):
            self.readers[i % self.nslots] = marker

    def emit_attn_a(next_kind):
        with ExitStack() as ph:
            P = ph.enter_context
            xn_sb = P(nc.sbuf_tensor("a_xn", [128, NDC * TSL], BF16))
            ctx_sb = P(nc.sbuf_tensor("a_ctx", [128, NDC * TSL], BF16))
            phiq_sb = P(nc.sbuf_tensor("a_phiq", [128, TSL], BF16))
            scrA = P(nc.sbuf_tensor("a_scrA", [128, TSL], F32))
            scrB = P(nc.sbuf_tensor("a_scrB", [128, TSL], F32))
            scrC = P(nc.sbuf_tensor("a_scrC", [128, TSL], F32))
            zeros_sb = P(nc.sbuf_tensor("a_zeros", [128, TSL], F32))
            carry_sb = P(nc.sbuf_tensor("a_carry", [128, NDC], F32))
            xsq_sb = P(nc.sbuf_tensor("a_xsq", [128, 2 * TSL], BF16))
            t_scr = P(nc.sbuf_tensor("a_st_t", [128, TSL], F32))
            u_scr = P(nc.sbuf_tensor("a_st_u", [128, TSL], BF16))
            wbuf = P(nc.sbuf_tensor("a_wbuf", [128, 6 * NDC * 128], BF16))
            psum = P(nc.psum_tensor("a_psum", [128, 8 * 512], F32))
            pool = PsumPool(psum, 8)
            ws = WStream(wbuf, 6, NDC * 128, ["w0", "w1", "w2", "w3", "w4", "w5"])

            em.instr("vector", "memset", zeros_sb[:], 0.0, waits=[consts_ready], inc="dv")
            em.instr("gpsimd", "memset", carry_sb[:], 0.0, inc="gp")
            carry_mark = [em.last("gp")] * NDC

            pending = {}

            def prefetch(sl, ec):
                for name, src in (("q", wqa), ("k", wka), ("v", wva)):
                    pending[(sl, ec, name)] = ws.load(src, ec, NDC)

            prefetch(0, 0)
            c_sl = ln_stats(0, pool, "plain", [x_loaded], xsq_sb, t_scr, None)
            c_next = None
            res_sl0 = None
            for sl in range(NSL):
                xn_ready = ln_apply(sl, xn_sb, "plain", c_sl, u_scr)
                if sl == 0:
                    c_sl = ln_stats(1, pool, "plain", [x_loaded], xsq_sb, t_scr,
                                    xn_ready)
                elif next_kind is not None:
                    c_next = ln_stats(0, pool, next_kind, [res_sl0], xsq_sb,
                                      t_scr, xn_ready)
                last_ctx = None
                for ec in range(NDC):
                    if ec + 1 < NDC:
                        prefetch(sl, ec + 1)
                    ps = {}
                    for name in ("q", "k", "v"):
                        i, loaded = pending.pop((sl, ec, name))
                        pp, wv, slot = pool.alloc(2)
                        for nh in range(2):
                            for dc in range(NDC):
                                mk = em.instr(
                                    "tensor", "matmul",
                                    pp[:, nh * 512 : nh * 512 + 512],
                                    ws.slot_ap(i)[:, dc * 128 : (dc + 1) * 128],
                                    xn_sb[:, dc * TSL + nh * 512 : dc * TSL + nh * 512 + 512],
                                    start=(dc == 0), stop=(dc == NDC - 1),
                                    waits=([loaded, xn_ready] + wv) if (nh == 0 and dc == 0) else (),
                                    inc="pe" if (nh == 1 and dc == NDC - 1) else None,
                                )
                        ws.set_reader(i, mk)
                        ps[name] = (pp, slot, mk)
                    ppq, slotq, mmq = ps["q"]
                    ppk, slotk, mmk = ps["k"]
                    ppv, slotv, mmv = ps["v"]
                    bq_ap = b_sb[:, ec : ec + 1]
                    bqs_ap = bqs_sb[:, ec : ec + 1]
                    bk_ap = b_sb[:, 16 + ec : 16 + ec + 1]
                    bv_ap = b_sb[:, 32 + ec : 32 + ec + 1]
                    em.instr("scalar", "activation", scrA[:], ppq[:, 0:TSL], AF.Relu,
                             scale=SCALE_A, bias=bqs_ap, waits=[mmq, em.last("dv")],
                             inc="ac")
                    rq_done = em.last("ac")
                    em.instr("vector", "tensor_scalar", scrB[:], ppq[:, 0:TSL],
                             bq_ap, 0.0, OP.add, OP.min, waits=[mmq, rq_done], inc="dv")
                    mnq_done = em.last("dv")
                    pool.add_reader(slotq, mnq_done)
                    em.instr("scalar", "activation", scrC[:], scrB[:], AF.Exp,
                             bias=LN_SCALE_A, waits=[mnq_done], inc="ac")
                    eq_done = em.last("ac")
                    em.instr("vector", "tensor_add", phiq_sb[:], scrA[:], scrC[:],
                             waits=[eq_done], inc="dv")
                    em.instr("scalar", "activation", scrA[:], ppk[:, 0:TSL], AF.Relu,
                             bias=bk_ap, waits=[mmk, em.last("dv")], inc="ac")
                    rk_done = em.last("ac")
                    em.instr("vector", "tensor_scalar", scrB[:], ppk[:, 0:TSL],
                             bk_ap, 0.0, OP.add, OP.min, waits=[mmk, rk_done], inc="dv")
                    mnk_done = em.last("dv")
                    pool.add_reader(slotk, mnk_done)
                    em.instr("scalar", "activation", scrC[:], scrB[:], AF.Exp,
                             waits=[mnk_done], inc="ac")
                    ek_done = em.last("ac")
                    em.instr("vector", "tensor_add", scrA[:], scrA[:], scrC[:],
                             waits=[ek_done], inc="dv")
                    em.instr("vector", "scalar_tensor_tensor", scrB[:], ppv[:, 0:TSL],
                             bv_ap, scrA[:], OP.add, OP.mult, waits=[mmv], inc="dv")
                    pool.add_reader(slotv, em.last("dv"))
                    em.instr("vector", "tensor_tensor_scan", scrC[:], scrB[:], zeros_sb[:],
                             carry_sb[:, ec : ec + 1], OP.add, OP.add,
                             waits=[carry_mark[ec]], inc="dv")
                    scan_done = em.last("dv")
                    if sl + 1 < NSL:
                        em.instr("scalar", "copy", carry_sb[:, ec : ec + 1],
                                 scrC[:, TSL - 1 : TSL], waits=[scan_done], inc="ac")
                        carry_mark[ec] = em.last("ac")
                    em.instr("vector", "tensor_mul",
                             ctx_sb[:, ec * TSL : (ec + 1) * TSL], phiq_sb[:], scrC[:],
                             inc="dv")
                    last_ctx = em.last("dv")
                wo_pending = {}
                wo_pending[0] = ws.load(woa, 0, NDC)
                wo_pending[1] = ws.load(woa, 1, NDC)
                for eo in range(NDC):
                    if eo + 2 < NDC:
                        wo_pending[eo + 2] = ws.load(woa, eo + 2, NDC)
                    elif eo == NDC - 2 and sl + 1 < NSL:
                        prefetch(sl + 1, 0)
                    i, loaded = wo_pending.pop(eo)
                    pp, wv, slot = pool.alloc(2)
                    for nh in range(2):
                        for ec in range(NDC):
                            mk = em.instr(
                                "tensor", "matmul",
                                pp[:, nh * 512 : nh * 512 + 512],
                                ws.slot_ap(i)[:, ec * 128 : (ec + 1) * 128],
                                ctx_sb[:, ec * TSL + nh * 512 : ec * TSL + nh * 512 + 512],
                                start=(ec == 0), stop=(ec == NDC - 1),
                                waits=([loaded, last_ctx] + wv) if (nh == 0 and ec == 0) else (),
                                inc="pe" if (nh == 1 and ec == NDC - 1) else None,
                            )
                    ws.set_reader(i, mk)
                    bo_ap = b_sb[:, 48 + eo : 48 + eo + 1]
                    em.instr("vector", "scalar_tensor_tensor", xc(eo, sl),
                             pp[:, 0:TSL], bo_ap, xc(eo, sl), OP.add, OP.add,
                             waits=[mk], inc="dv")
                    pool.add_reader(slot, em.last("dv"))
                if sl == 0:
                    res_sl0 = em.last("dv")
        return c_next, em.last("dv")

    def emit_ffn(c_sl, prev_res_sl1, final, tag, next_kind):
        with ExitStack() as ph:
            P = ph.enter_context
            psum = P(nc.psum_tensor(tag + "_psum", [128, 8 * 512], F32))
            pool = PsumPool(psum, 6)
            cs_ps = psum[:, 6 * 512 : 8 * 512]
            gu_done = None
            cs_reader = None
            stage_mk = [None, None]
            c_next = None
            res_sl0 = None
            for sl in range(NSL):
                with ExitStack() as slsc:
                    Q = slsc.enter_context
                    xnsc = ExitStack()
                    X_ = xnsc.enter_context
                    xn_sb = X_(nc.sbuf_tensor(f"{tag}{sl}_xn", [128, NDC * TSL], BF16))
                    with ExitStack() as lnsc:
                        L = lnsc.enter_context
                        t_scr = L(nc.sbuf_tensor(f"{tag}{sl}_st_t", [128, TSL], F32))
                        u_scr = L(nc.sbuf_tensor(f"{tag}{sl}_st_u", [128, TSL], BF16))
                        xsq_sb = L(nc.sbuf_tensor(f"{tag}{sl}_xsq", [128, 2 * TSL], BF16))
                        xn_ready = ln_apply(sl, xn_sb, "plain", c_sl, u_scr)
                        lnguard = xn_ready
                        if sl == 0:
                            c_sl = lnguard = ln_stats(1, pool, "plain",
                                                      [prev_res_sl1], xsq_sb,
                                                      t_scr, xn_ready)
                        elif next_kind is not None:
                            c_next = lnguard = ln_stats(0, pool, next_kind,
                                                        [res_sl0], xsq_sb,
                                                        t_scr, xn_ready)
                    act_sb = Q(nc.sbuf_tensor(f"{tag}{sl}_act", [128, NHC * TSL], BF16,
                                              side="right"))
                    with ExitStack() as gusc:
                        G = gusc.enter_context
                        gu_buf = G(nc.sbuf_tensor(f"{tag}{sl}_gu", [128, 6 * 8 * 128], BF16))
                        g_scr = G(nc.sbuf_tensor(f"{tag}{sl}_g", [128, 2 * TSL], BF16))
                        gws = WStream(gu_buf, 6, 8 * 128,
                                      ["w0", "w1", "w2", "w3", "w4", "w5"])
                        gws.readers = [lnguard] * 6
                        gu_jobs = []
                        for hc_ in range(NHC):
                            for src_ in (wg, wu):
                                for hf in range(2):
                                    gu_jobs.append(src_[hc_, hf * 8 : hf * 8 + 8])
                        gu_pend = {}
                        gu_next = [0]

                        def gu_load():
                            j = gu_next[0]
                            gu_next[0] += 1
                            gu_pend[j] = gws.load_ap(gu_jobs[j], 8)

                        for _ in range(6):
                            gu_load()
                        act_mk = {}
                        g_war = [None, None]

                        def emit_cs_act(hcl):
                            for nh in range(2):
                                em.instr("tensor", "matmul",
                                         cs_ps[:, nh * 512 : nh * 512 + 512],
                                         ones_sb[:, :],
                                         act_sb[:, hcl * TSL + nh * 512 :
                                                hcl * TSL + nh * 512 + 512],
                                         start=(hcl == 0), stop=(hcl == NHC - 1),
                                         waits=[act_mk[hcl], cs_reader]
                                         if hcl == 0 else [act_mk[hcl]],
                                         inc="pe" if (hcl == NHC - 1 and nh == 1) else None)

                        for hc in range(NHC):
                            pps = {}
                            for nmi, name in enumerate(("g", "u")):
                                j0 = (hc * 2 + nmi) * 2
                                i0, ld0 = gu_pend.pop(j0)
                                i1, ld1 = gu_pend.pop(j0 + 1)
                                pp, wv, slot = pool.alloc(2)
                                for nh in range(2):
                                    for dc in range(NDC):
                                        ii = i0 if dc < 8 else i1
                                        mk = em.instr(
                                            "tensor", "matmul",
                                            pp[:, nh * 512 : nh * 512 + 512],
                                            gws.slot_ap(ii)[:, (dc % 8) * 128 : (dc % 8 + 1) * 128],
                                            xn_sb[:, dc * TSL + nh * 512 : dc * TSL + nh * 512 + 512],
                                            start=(dc == 0), stop=(dc == NDC - 1),
                                            waits=([ld0, ld1, xn_ready] + wv)
                                            if (nh == 0 and dc == 0) else (),
                                            inc="pe" if (nh == 1 and dc == NDC - 1) else None,
                                        )
                                ws_mk = mk
                                gws.set_reader(i0, ws_mk)
                                gws.set_reader(i1, ws_mk)
                                pps[name] = (pp, slot, ws_mk)
                                for _ in range(2):
                                    if gu_next[0] < len(gu_jobs):
                                        gu_load()
                            ppg, slotg, mmg = pps["g"]
                            ppu, slotu, mmu = pps["u"]
                            gp = g_scr[:, (hc % 2) * TSL : (hc % 2 + 1) * TSL]
                            em.instr("scalar", "activation", gp, ppg[:, 0:TSL],
                                     AF.Silu, scale=sgu_sb[:, hc : hc + 1],
                                     waits=[mmg, g_war[hc % 2]], inc="ac")
                            g_done = em.last("ac")
                            pool.add_reader(slotg, g_done)
                            em.instr("vector", "scalar_tensor_tensor",
                                     act_sb[:, hc * TSL : (hc + 1) * TSL],
                                     gp, sgu_sb[:, NHC + hc : NHC + hc + 1],
                                     ppu[:, 0:TSL], OP.mult, OP.mult,
                                     waits=[mmu, g_done], inc="dv")
                            act_mk[hc] = g_war[hc % 2] = em.last("dv")
                            pool.add_reader(slotu, act_mk[hc])
                            if hc >= 2:
                                emit_cs_act(hc - 2)
                        for hcl in (NHC - 2, NHC - 1):
                            emit_cs_act(hcl)
                        gu_done = em.last("pe")
                    xnsc.close()
                    with ExitStack() as dnsc:
                        DN = dnsc.enter_context
                        dn_buf = DN(nc.sbuf_tensor(f"{tag}{sl}_dn", [128, 2 * NHC * 128], BF16))
                        ca_row = DN(nc.sbuf_tensor(f"{tag}{sl}_carow", [1, TSL], BF16))
                        if final:
                            stage = DN(nc.sbuf_tensor(f"{tag}{sl}_stage", [128, 2 * TSL], F32))
                        dws = WStream(dn_buf, 2, NHC * 128, ["w2", "w3"])
                        dws.readers = [gu_done, gu_done]
                        em.instr("vector", "tensor_copy", ca_row[0:1, :],
                                 cs_ps[0:1, 0:TSL], waits=[gu_done], inc="dv")
                        ca_done = em.last("dv")
                        cs_reader = ca_done
                        dn_pending = {0: dws.load(wd, 0, NHC)}
                        for dout in range(NDC):
                            if dout + 1 < NDC:
                                dn_pending[dout + 1] = dws.load(wd, dout + 1, NHC)
                            i, loaded = dn_pending.pop(dout)
                            pp, wv, slot = pool.alloc(2)
                            for nh in range(2):
                                for hc in range(NHC):
                                    em.instr(
                                        "tensor", "matmul",
                                        pp[:, nh * 512 : nh * 512 + 512],
                                        dws.slot_ap(i)[:, hc * 128 : (hc + 1) * 128],
                                        act_sb[:, hc * TSL + nh * 512 : hc * TSL + nh * 512 + 512],
                                        start=(hc == 0), stop=False,
                                        waits=([loaded] + wv) if (nh == 0 and hc == 0) else (),
                                    )
                                mk = em.instr(
                                    "tensor", "matmul",
                                    pp[:, nh * 512 : nh * 512 + 512],
                                    neg128_sb[0:1, :],
                                    ca_row[0:1, nh * 512 : nh * 512 + 512],
                                    start=False, stop=True,
                                    waits=[ca_done] if nh == 0 else (),
                                    inc="pe" if nh == 1 else None,
                                )
                            dws.set_reader(i, mk)
                            sd_ap = sdn_sb[:, dout : dout + 1]
                            if final:
                                sslot = stage[:, (dout % 2) * TSL : (dout % 2 + 1) * TSL]
                                em.instr("vector", "scalar_tensor_tensor", sslot,
                                         pp[:, 0:TSL], sd_ap, xc(dout, sl),
                                         OP.mult, OP.add,
                                         waits=[mk, stage_mk[dout % 2]], inc="dv")
                                pool.add_reader(slot, em.last("dv"))
                                osem = "oA" if dout % 2 == 0 else "oB"
                                em.instr("sync", "dma_start",
                                         y[dout, :, sl * TSL : (sl + 1) * TSL], sslot,
                                         waits=[em.last("dv")], inc=osem, incn=16)
                                stage_mk[dout % 2] = em.last(osem)
                            else:
                                em.instr("vector", "scalar_tensor_tensor",
                                         xc(dout, sl), pp[:, 0:TSL], sd_ap,
                                         xc(dout, sl), OP.mult, OP.add,
                                         waits=[mk], inc="dv")
                                pool.add_reader(slot, em.last("dv"))
                        if sl == 0:
                            res_sl0 = em.last("dv")
            if final:
                em.wait("sync", em.last("oA"), em.last("oB"))
            return c_next, em.last("dv")

    def emit_attn_b(c_sl, prev_res_sl1, next_kind):
        with ExitStack() as ph:
            P = ph.enter_context
            maskd_sb = P(nc.sbuf_tensor("b_maskd", [128, 128], BF16))
            maskw_sb = P(nc.sbuf_tensor("b_maskw", [128, 128], BF16))
            i32_sb = P(nc.sbuf_tensor("b_i32", [128, 128], F32))
            cos_sb = P(nc.sbuf_tensor("b_cos", [128, TSL], BF16))
            sin_sb = P(nc.sbuf_tensor("b_sin", [128, TSL], BF16))
            krot_sb = P(nc.sbuf_tensor("b_krot", [128, NKV * S], BF16))
            vtm_sb = P(nc.sbuf_tensor("b_vtm", [128, NKV * S], BF16))
            wsbuf = P(nc.sbuf_tensor("b_wsbuf", [128, 2 * NDC * 128], BF16))
            qrot_sb = P(nc.sbuf_tensor("b_qrot", [128, 4 * TSL], BF16))
            vbf_sb = P(nc.sbuf_tensor("b_vbf", [128, TSL], F32))
            u1_sb = P(nc.sbuf_tensor("b_u1", [128, TSL], BF16))
            u2_sb = P(nc.sbuf_tensor("b_u2", [128, TSL], BF16))
            u2s_sb = P(nc.sbuf_tensor("b_u2s", [128, TSL], BF16))
            attnT_sb = P(nc.sbuf_tensor("b_attnT", [128, 4 * 512], BF16))
            rden_sb = P(nc.sbuf_tensor("b_rden", [1, TSL], BF16))
            zrow_sb = P(nc.sbuf_tensor("b_zrow", [128, 512], BF16))
            bcast_sb = P(nc.sbuf_tensor("b_bcast", [128, TSL], BF16))
            arena = P(nc.psum_tensor("b_arena", [128, 8 * 512], F32))
            rotb = [arena[:, 0:512], arena[:, 512:1024]]
            proj_ps = arena[:, 2 * 512 : 4 * 512]
            av_ps = arena[:, 4 * 512 : 6 * 512]
            den_ps = arena[:, 6 * 512 : 8 * 512]

            em.instr("sync", "dma_start", maskd_sb[:], maskd_c, waits=[consts_ready],
                     inc="dC", incn=16)
            em.instr("sync", "dma_start", maskw_sb[:], maskw_c, inc="dC", incn=16)
            em.instr("sync", "dma_start", i32_sb[:], ident32_c, inc="dC", incn=16)
            masks_ok = em.last("dC")
            em.wait("vector", masks_ok)
            em.wait("tensor", masks_ok)

            em.instr("vector", "memset", zrow_sb[:], 0.0, inc="dv")
            zrow_ok = em.last("dv")
            ws = WStream(wsbuf, 2, NDC * 128, ["w0", "w1"])
            qr_mk = [None] * 4
            qr_war = [None] * 4
            kr_done = {}
            vtm_ok = {}
            vbf_war = None
            rope_done = None
            attnT_war = [None] * 4
            rot_war = [None, None]
            proj_war = None
            av_war = None
            den_war = None
            rden_war = None
            bcast_war = None
            ao_last = None
            c_next = None
            res_sl0 = None
            carry_readers = None

            for sl in range(NSL):
                qt_lo, qt_hi = sl * 8, sl * 8 + 7
                em.instr("sync", "dma_start", cos_sb[:],
                         cos2_c[:, sl * TSL : (sl + 1) * TSL],
                         waits=[rope_done or consts_ready], inc="dC", incn=16)
                em.instr("sync", "dma_start", sin_sb[:],
                         sin2_c[:, sl * TSL : (sl + 1) * TSL], inc="dC", incn=16)
                cs_ok = em.last("dC")
                jobs = []
                for kvh in range(NKV):
                    jobs.append((wkb, kvh))
                    jobs.append((wvb, kvh))
                for h in range(NH):
                    jobs.append((wqb, h))
                for eo in range(NDC):
                    jobs.append((wob, eo))
                pend = {}
                for j in range(2):
                    pend[j] = ws.load(jobs[j][0], jobs[j][1], NDC)
                nextload = [2]

                def consume(j):
                    return pend.pop(j)

                def after_use(i, mk):
                    ws.set_reader(i, mk)
                    if nextload[0] < len(jobs):
                        j = nextload[0]
                        nextload[0] += 1
                        pend[j] = ws.load(jobs[j][0], jobs[j][1], NDC)

                setup_pool = PsumPool(arena, 8)
                if carry_readers is not None:
                    setup_pool.readers = [list(r) for r in carry_readers]

                with ExitStack() as slsc:
                    Q = slsc.enter_context
                    xn_sb = Q(nc.sbuf_tensor(f"b{sl}_xn", [128, NDC * TSL], BF16))
                    lnsc = ExitStack()
                    L = lnsc.enter_context
                    t_scr = L(nc.sbuf_tensor(f"b{sl}_st_t", [128, TSL], F32))
                    u_scr = L(nc.sbuf_tensor(f"b{sl}_st_u", [128, TSL], BF16))
                    xsq_sb = L(nc.sbuf_tensor(f"b{sl}_xsq", [128, 2 * TSL], BF16))
                    xn_ready = ln_apply(sl, xn_sb, "attnb", c_sl, u_scr)

                    def proj_setup(jidx):
                        i, loaded = consume(jidx)
                        pp, wv, slot = setup_pool.alloc(2)
                        for nh in range(2):
                            for dc in range(NDC):
                                mk = em.instr(
                                    "tensor", "matmul",
                                    pp[:, nh * 512 : nh * 512 + 512],
                                    ws.slot_ap(i)[:, dc * 128 : (dc + 1) * 128],
                                    xn_sb[:, dc * TSL + nh * 512 : dc * TSL + nh * 512 + 512],
                                    start=(dc == 0), stop=(dc == NDC - 1),
                                    waits=([loaded, xn_ready] + wv)
                                    if (nh == 0 and dc == 0) else (),
                                    inc="pe" if (nh == 1 and dc == NDC - 1) else None,
                                )
                        return pp, slot, mk, i

                    def rope(pp, dst_ap, mm_mk, war=()):
                        em.instr("vector", "tensor_mul", u1_sb[:], pp[:, 0:TSL],
                                 cos_sb[:], waits=[mm_mk, cs_ok], inc="dv")
                        em.instr("vector", "tensor_mul", u2_sb[:], pp[:, 0:TSL],
                                 sin_sb[:], inc="dv")
                        em.instr("vector", "tensor_copy", u2s_sb[0:64, :],
                                 u2_sb[64:128, :], inc="dv")
                        em.instr("vector", "tensor_copy", u2s_sb[64:128, :],
                                 u2_sb[0:64, :], inc="dv")
                        em.instr("vector", "tensor_add", dst_ap[:, :],
                                 u1_sb[:], u2s_sb[:], waits=list(war), inc="dv")
                        return em.last("dv")

                    for kvh in range(NKV):
                        ppk, slotk, mmk, ik = proj_setup(2 * kvh)
                        kdst = krot_sb[:, kvh * S + sl * TSL : kvh * S + (sl + 1) * TSL]
                        kr = rope(ppk, kdst, mmk)
                        kr_done[(kvh, sl)] = kr
                        setup_pool.add_reader(slotk, kr)
                        after_use(ik, kr)
                        ppv, slotv, mmv, iv = proj_setup(2 * kvh + 1)
                        em.instr("scalar", "copy", vbf_sb[:], ppv[:, 0:TSL],
                                 waits=[mmv, vbf_war], inc="ac")
                        vb_ok = em.last("ac")
                        setup_pool.add_reader(slotv, vb_ok)
                        after_use(iv, vb_ok)
                        tps, twv, tslot = setup_pool.alloc(2)
                        for j in range(8):
                            tmk = em.instr("tensor", "transpose",
                                           tps[:, j * 128 : (j + 1) * 128],
                                           vbf_sb[:, j * 128 : (j + 1) * 128],
                                           i32_sb[:],
                                           waits=([vb_ok] + twv) if j == 0 else (),
                                           inc="pe" if j == 7 else None)
                        em.instr("scalar", "copy",
                                 vtm_sb[:, (kvh * NDC + sl * 8) * 128 :
                                        (kvh * NDC + sl * 8 + 8) * 128],
                                 tps[:, 0:1024], waits=[tmk], inc="ac")
                        vt_ok = em.last("ac")
                        setup_pool.add_reader(tslot, vt_ok)
                        vtm_ok[(kvh, sl)] = vt_ok
                        vbf_war = vt_ok

                    if sl == 0:
                        c_sl = ln_stats(1, setup_pool, "attnb", [prev_res_sl1],
                                        xsq_sb, t_scr, xn_ready)
                    elif next_kind is not None:
                        c_next = ln_stats(0, setup_pool, next_kind, [res_sl0],
                                          xsq_sb, t_scr, xn_ready)
                    lnsc.close()
                    attnout_sb = Q(nc.sbuf_tensor(f"b{sl}_ao", [128, NH * TSL], BF16,
                                                  side="right"))
                    sr = setup_pool.readers
                    rot_war = [sr[0][-1] if sr[0] else None,
                               sr[1][-1] if sr[1] else None]
                    proj_war = sr[3][-1] if sr[3] else (sr[2][-1] if sr[2] else None)
                    pw2 = sr[2][-1] if sr[2] else None
                    av_war = sr[5][-1] if sr[5] else (sr[4][-1] if sr[4] else None)
                    aw2 = sr[4][-1] if sr[4] else None
                    den_war = sr[7][-1] if sr[7] else (sr[6][-1] if sr[6] else None)
                    dw2 = sr[6][-1] if sr[6] else None

                    qpend = {}
                    rot_i = [0]
                    buf_i = [0]

                    def qproj_start(h):
                        qpend[h] = consume(8 + h)

                    def qproj_part(h, part, extra=()):
                        i, loaded = qpend[h]
                        nh, half = part // 2, part % 2
                        for dc in range(half * 8, half * 8 + 8):
                            mk = em.instr(
                                "tensor", "matmul",
                                proj_ps[:, nh * 512 : nh * 512 + 512],
                                ws.slot_ap(i)[:, dc * 128 : (dc + 1) * 128],
                                xn_sb[:, dc * TSL + nh * 512 : dc * TSL + nh * 512 + 512],
                                start=(dc == 0), stop=(dc == NDC - 1),
                                waits=([loaded, xn_ready, proj_war] + list(extra))
                                if (part == 0 and dc == 0) else (),
                                inc="pe" if part == 3 and dc == NDC - 1 else None,
                            )
                        if part == 3:
                            qs = h % 4
                            qr_mk[qs] = rope(proj_ps, qrot_sb[:, qs * TSL : (qs + 1) * TSL],
                                             mk, war=[qr_war[qs]])
                            after_use(i, qr_mk[qs])
                            return qr_mk[qs]
                        return None

                    qproj_start(0)
                    pw = qproj_part(0, 0, extra=[pw2] if pw2 else [])
                    for p in range(1, 4):
                        pw = qproj_part(0, p)
                    proj_war = pw
                    qproj_start(1)
                    for p in range(4):
                        pw = qproj_part(1, p)
                    proj_war = pw

                    pending_finish = [None]
                    for h in range(NH):
                        kvh = h // 4
                        qs = h % 4
                        chunks = []
                        for kt in range(max(0, qt_lo - 4), qt_hi + 1):
                            qlo_t = max(kt, qt_lo)
                            qhi_t = min(kt + 4, qt_hi)
                            if qhi_t - qlo_t + 1 == 5:
                                chunks.append((kt, qlo_t, qlo_t + 3))
                                chunks.append((kt, qhi_t, qhi_t))
                            else:
                                chunks.append((kt, qlo_t, qhi_t))
                        nxt = h + 2
                        proj_plan = {}
                        if nxt < NH:
                            step = max(1, len(chunks) // 5)
                            for p in range(4):
                                proj_plan[min((p + 1) * step, len(chunks) - 1)] = p
                        pending = []
                        den_mk = av_mk = None

                        def emit_denav(pc):
                            nonlocal den_mk, av_mk
                            kt, qlo, qhi, rb, buf, t_waits = pc
                            sl_kt = 0 if kt < 8 else 1
                            strip = attnT_sb[:, buf * 512 : buf * 512 + (qhi - qlo + 1) * 128]
                            groups = []
                            for qtg in range(qlo, qhi + 1):
                                bank = 0 if qtg < qt_lo + 4 else 1
                                if groups and groups[-1][2] == bank:
                                    groups[-1][1] = qtg
                                else:
                                    groups.append([qtg, qtg, bank])
                            for g_lo, g_hi, g_bank in groups:
                                sp = (g_bank == 0 and kt == qt_lo + 3) or \
                                     (g_bank == 1 and kt == qt_hi)
                                oc = slice((g_lo - qt_lo) * 128, (g_hi + 1 - qt_lo) * 128)
                                mc = slice((g_lo - qlo) * 128, (g_hi + 1 - qlo) * 128)
                                den_mk = em.instr(
                                    "tensor", "matmul", den_ps[0:1, oc],
                                    ones_sb[:, 0:1], strip[:, mc],
                                    start=False, stop=sp, waits=list(t_waits),
                                    inc="pe")
                                av_mk = em.instr(
                                    "tensor", "matmul", av_ps[:, oc],
                                    vtm_sb[:, (kvh * NDC + kt) * 128 :
                                           (kvh * NDC + kt + 1) * 128],
                                    strip[:, mc],
                                    start=False, stop=sp,
                                    waits=[vtm_ok[(kvh, sl_kt)]], inc="pe")
                            attnT_war[buf] = av_mk

                        for ci, (kt, qlo, qhi) in enumerate(chunks):
                            nqc = qhi - qlo + 1
                            sl_kt = 0 if kt < 8 else 1
                            rb = rot_i[0] % 2
                            rot_i[0] += 1
                            buf_i[0] += 1
                            buf = buf_i[0] % 4
                            smk = em.instr(
                                "tensor", "matmul", rotb[rb][:, 0 : nqc * 128],
                                krot_sb[:, kvh * S + kt * 128 : kvh * S + (kt + 1) * 128],
                                qrot_sb[:, qs * TSL + (qlo - qt_lo) * 128 :
                                        qs * TSL + (qhi + 1 - qt_lo) * 128],
                                start=True, stop=True,
                                waits=[kr_done[(kvh, sl_kt)], qr_mk[qs], rot_war[rb]],
                                inc="pe")
                            qr_war[qs] = smk
                            strip = attnT_sb[:, buf * 512 : buf * 512 + nqc * 128]
                            em.instr("scalar", "activation", strip,
                                     rotb[rb][:, 0 : nqc * 128], AF.Exp,
                                     scale=SCALE_A, bias=-25.0,
                                     waits=[smk, attnT_war[buf]], inc="ac")
                            e_mk = em.last("ac")
                            rot_war[rb] = e_mk
                            t_waits = [e_mk]
                            if qlo == kt:
                                em.instr("vector", "tensor_mul",
                                         attnT_sb[:, buf * 512 : buf * 512 + 128],
                                         attnT_sb[:, buf * 512 : buf * 512 + 128],
                                         maskd_sb[:], waits=[e_mk], inc="dv")
                                t_waits.append(em.last("dv"))
                            if qhi == kt + 4:
                                em.instr("vector", "tensor_mul",
                                         attnT_sb[:, buf * 512 + (nqc - 1) * 128 :
                                                  buf * 512 + nqc * 128],
                                         attnT_sb[:, buf * 512 + (nqc - 1) * 128 :
                                                  buf * 512 + nqc * 128],
                                         maskw_sb[:], waits=[e_mk], inc="dv")
                                t_waits.append(em.last("dv"))
                            if ci == 0:
                                if pending_finish[0] is not None:
                                    pending_finish[0]()
                                    pending_finish[0] = None
                                ow = [zrow_ok, den_war, dw2]
                                for bk in range(2):
                                    em.instr("tensor", "matmul",
                                             den_ps[0:1, bk * 512 : bk * 512 + 512],
                                             ones_sb[0:1, 0:1], zrow_sb[0:1, :],
                                             start=True, stop=False,
                                             waits=ow if bk == 0 else (), inc="pe")
                                ow = [av_war, aw2]
                                for bk in range(2):
                                    em.instr("tensor", "matmul",
                                             av_ps[:, bk * 512 : bk * 512 + 512],
                                             ones_sb[:, :], zrow_sb[:, :],
                                             start=True, stop=False,
                                             waits=ow if bk == 0 else (), inc="pe")
                                dw2 = aw2 = None
                            if ci in proj_plan:
                                p = proj_plan[ci]
                                if p == 0:
                                    qproj_start(nxt)
                                mkq = qproj_part(nxt, p)
                                if mkq is not None:
                                    proj_war = mkq
                            if len(pending) >= 2:
                                emit_denav(pending.pop(0))
                            pending.append((kt, qlo, qhi, rb, buf, t_waits))
                        while pending:
                            emit_denav(pending.pop(0))
                        em.instr("vector", "reciprocal", rden_sb[0:1, :],
                                 den_ps[0:1, 0:TSL],
                                 waits=[den_mk, rden_war], inc="dv", lowp=True)
                        rc_mk = em.last("dv")
                        den_war = rc_mk

                        def finish(h=h, av_mk=av_mk, rc_mk=rc_mk):
                            nonlocal rden_war, bcast_war, av_war, ao_last
                            bmks = []
                            for nh2 in range(2):
                                rb = rot_i[0] % 2
                                rot_i[0] += 1
                                bmk = em.instr(
                                    "tensor", "matmul", rotb[rb][:, 0:512],
                                    ones_sb[0:1, :],
                                    rden_sb[0:1, nh2 * 512 : nh2 * 512 + 512],
                                    start=True, stop=True,
                                    waits=[rc_mk, rot_war[rb]],
                                    inc="pe")
                                rden_war = bmk
                                em.instr("vector", "tensor_copy",
                                         bcast_sb[:, nh2 * 512 : nh2 * 512 + 512],
                                         rotb[rb][:, 0:512],
                                         waits=[bmk, bcast_war], inc="dv")
                                rot_war[rb] = em.last("dv")
                                bmks.append(em.last("dv"))
                            em.instr("vector", "tensor_mul",
                                     attnout_sb[:, h * TSL : (h + 1) * TSL],
                                     av_ps[:, 0:TSL], bcast_sb[:],
                                     waits=[av_mk] + bmks, inc="dv")
                            ao_last = av_war = bcast_war = em.last("dv")

                        pending_finish[0] = finish
                    pending_finish[0]()
                    pending_finish[0] = None

                    wo_pool = PsumPool(arena, 4)
                    wo_pool.readers = [[rot_war[0]], [rot_war[1]],
                                       [proj_war], [proj_war]]
                    for eo in range(NDC):
                        i, loaded = consume(24 + eo)
                        pp, wv, slot = wo_pool.alloc(2)
                        for nh in range(2):
                            for ei in range(NH):
                                mk = em.instr(
                                    "tensor", "matmul",
                                    pp[:, nh * 512 : nh * 512 + 512],
                                    ws.slot_ap(i)[:, ei * 128 : (ei + 1) * 128],
                                    attnout_sb[:, ei * TSL + nh * 512 :
                                               ei * TSL + nh * 512 + 512],
                                    start=(ei == 0), stop=(ei == NH - 1),
                                    waits=([loaded, ao_last] + wv)
                                    if (nh == 0 and ei == 0) else (),
                                    inc="pe" if (nh == 1 and ei == NH - 1) else None,
                                )
                        after_use(i, mk)
                        em.instr("vector", "tensor_add", xc(eo, sl), xc(eo, sl),
                                 pp[:, 0:TSL], waits=[mk], inc="dv")
                        wo_pool.add_reader(slot, em.last("dv"))
                    if sl == 0:
                        res_sl0 = em.last("dv")
                    rope_done = em.last("dv")
                    wr = wo_pool.readers
                    carry_readers = [wr[0], wr[1], wr[2], wr[3],
                                     [av_war], [av_war], [den_war], [den_war]]
        return c_next, em.last("dv")

    def emit_flush(x_ready):
        with ExitStack() as ph:
            P = ph.enter_context
            stage = P(nc.sbuf_tensor("fl_stage", [128, 2 * TSL], F32))
            dma_mk = [None] * 2
            k = 0
            for dc in range(NDC):
                for sl in range(NSL):
                    sslot = stage[:, (k % 2) * TSL : (k % 2 + 1) * TSL]
                    em.instr("vector", "tensor_copy", sslot, xc(dc, sl),
                             waits=[x_ready, dma_mk[k % 2]], inc="dv")
                    osem = "oA" if k % 2 == 0 else "oB"
                    em.instr("sync", "dma_start",
                             y[dc, :, sl * TSL : (sl + 1) * TSL], sslot,
                             waits=[em.last("dv")], inc=osem, incn=16)
                    dma_mk[k % 2] = em.last(osem)
                    k += 1
            em.wait("sync", em.last("oA"), em.last("oB"))

    def phase_gate():
        snap = [em.last(s) for s in ("pe", "dv", "ac")]
        for e in ("gpsimd", "sync", "tensor", "vector", "scalar"):
            em.wait(e, *snap)

    c, res1 = emit_attn_a("plain" if n_phases >= 2 else None)
    if n_phases >= 2:
        phase_gate()
        c, res1 = emit_ffn(c, res1, final=False, tag="f1",
                           next_kind="attnb" if n_phases >= 3 else None)
    if n_phases >= 3:
        phase_gate()
        c, res1 = emit_attn_b(c, res1,
                              next_kind="plain" if n_phases >= 4 else None)
    if n_phases >= 4:
        phase_gate()
        emit_ffn(c, res1, final=True, tag="f2", next_kind=None)
    else:
        phase_gate()
        emit_flush(res1)

    with nc.Block() as block:
        @block.sync
        def _(e):
            for fn in em.ops["sync"]:
                fn(e)

        @block.gpsimd
        def _(e):
            for fn in em.ops["gpsimd"]:
                fn(e)

        @block.tensor
        def _(e):
            for fn in em.ops["tensor"]:
                fn(e)

        @block.vector
        def _(e):
            for fn in em.ops["vector"]:
                fn(e)

        @block.scalar
        def _(e):
            for fn in em.ops["scalar"]:
                fn(e)

    st.close()
    return nc


def _prep_shared(ins):
    f32 = np.float32
    out = {}
    t44 = lambda a, r, c: np.ascontiguousarray(
        a.reshape(r, 128, c, 128).transpose(2, 0, 1, 3))
    out["wqa"] = t44(np.ascontiguousarray(ins["L0_attn_a_Wq"].T), NDC, NDC)
    out["wka"] = t44(np.ascontiguousarray(ins["L0_attn_a_Wk"].T), NDC, NDC)
    out["wva"] = t44(np.ascontiguousarray(ins["L0_attn_a_Wv"].T), NDC, NDC)
    out["woa"] = t44(np.ascontiguousarray(ins["L0_attn_a_Wo"].T), NDC, NDC)
    cols = lambda v, n: np.ascontiguousarray(v.reshape(n, 128).T, dtype=f32)
    out["biases"] = np.concatenate(
        [cols(ins["L0_attn_a_b" + k], 16) for k in "qkvo"], axis=1)
    out["wg"] = t44(np.ascontiguousarray(ins["L0_ffn_wq_gate"].T), NDC, NHC).astype(np.uint8)
    out["wu"] = t44(np.ascontiguousarray(ins["L0_ffn_wq_up"].T), NDC, NHC).astype(np.uint8)
    out["wd"] = t44(np.ascontiguousarray(ins["L0_ffn_wq_down"].T), NHC, NDC).astype(np.uint8)
    out["sgu"] = np.concatenate(
        [cols(ins["L0_ffn_s_gate"], NHC), cols(ins["L0_ffn_s_up"], NHC)], axis=1)
    out["sdn"] = cols(ins["L0_ffn_s_down"], NDC)
    perm = np.concatenate([np.arange(0, 128, 2), np.arange(1, 128, 2)])
    qp = ins["L0_attn_b_wq"].reshape(D, NH, 128)[:, :, perm]
    kp = ins["L0_attn_b_wk"].reshape(D, NKV, 128)[:, :, perm]
    vp = ins["L0_attn_b_wv"].reshape(D, NKV, 128)
    out["wqb"] = np.ascontiguousarray(
        qp.transpose(1, 0, 2).reshape(NH, NDC, 128, 128))
    out["wkb"] = np.ascontiguousarray(
        kp.transpose(1, 0, 2).reshape(NKV, NDC, 128, 128))
    out["wvb"] = np.ascontiguousarray(
        vp.transpose(1, 0, 2).reshape(NKV, NDC, 128, 128))
    out["wob"] = t44(ins["L0_attn_b_wo"], NDC, NH)
    out["lnw"] = cols(ins["L0_attn_b_ln_w"], NDC)
    out["ones_c"] = np.ones((128, 128), dtype=ml_dtypes.bfloat16)
    out["ident_c"] = np.eye(128, dtype=ml_dtypes.bfloat16)
    out["neg128_c"] = np.full((1, 128), -128.0, dtype=ml_dtypes.bfloat16)
    kk = np.arange(128)[:, None]
    qq = np.arange(128)[None, :]
    out["maskd_c"] = (kk <= qq).astype(ml_dtypes.bfloat16)
    out["maskw_c"] = (kk >= qq).astype(ml_dtypes.bfloat16)
    out["ident32_c"] = np.eye(128, dtype=f32)
    invf = 1.0 / 10000.0 ** (np.arange(0, HD, 2, dtype=np.float64) / HD)
    ang = np.arange(S, dtype=np.float64)[:, None] * invf[None, :]
    cosT = np.cos(ang).T.astype(ml_dtypes.bfloat16)
    sinT = np.sin(ang).T.astype(ml_dtypes.bfloat16)
    out["cos2_c"] = np.ascontiguousarray(np.concatenate([cosT, cosT], axis=0))
    out["sin2_c"] = np.ascontiguousarray(np.concatenate([sinT, -sinT], axis=0))
    return out


def run(inputs, n_phases=4, trace=False):
    shared = _prep_shared(inputs)
    x = np.asarray(inputs["L0_attn_a_x"], dtype=np.float32)
    B = x.shape[0]
    in_maps = []
    for b in range(B):
        m = dict(shared)
        m["xT"] = np.ascontiguousarray(x[b].T).reshape(NDC, 128, S)
        in_maps.append(m)
    nc = build_program(n_phases=n_phases)
    res = run_bass_kernel_spmd(nc, in_maps, core_ids=list(range(8)), trace=trace)
    outs = []
    for b in range(B):
        yb = res.results[b]["y"]
        outs.append(yb.reshape(D, S).T)
    return np.stack(outs, axis=0), res


def kernel(**inputs):
    out, _ = run(inputs, n_phases=4)
    return out.astype(np.float32)


# version 22 (final)
# speedup vs baseline: 1.1835x; 1.1835x over previous
import sys
from contextlib import ExitStack

import numpy as np
import ml_dtypes

sys.path.insert(0, "/opt/trn_rl_repo")
import concourse.bass as bass
import concourse.mybir as mybir
from concourse.bass_utils import run_bass_kernel_spmd
from concourse.alu_op_type import AluOpType as OP

F32 = mybir.dt.float32
BF16 = mybir.dt.bfloat16
I32 = mybir.dt.int32
AF = mybir.ActivationFunctionType
AX = mybir.AxisListType

D = 2048
S = 2048
NDC = D // 128
TSL = 1024
NSL = S // TSL
HID = 5632
NHC = HID // 128
NH = 16
NKV = 4
HD = 128
WIN = 512
SCALE_A = 128.0 ** -0.5
LN_SCALE_A = float(np.log(SCALE_A))

ENGINES = ("sync", "gpsimd", "tensor", "vector", "scalar")

DBG = {"enable": False}


class Em:

    def __init__(self, nc):
        self.nc = nc
        self.ops = {e: [] for e in ENGINES}
        self.sems = {}
        self.cnt = {}
        self.waited = {e: {} for e in ENGINES}

    def add_sem(self, name, handle):
        self.sems[name] = handle
        self.cnt[name] = 0

    def instr(self, eng, method, *args, waits=(), inc=None, incn=1, lowp=False, **kw):
        ws = []
        for wv in waits:
            if wv is None:
                continue
            s, v = wv
            if v is None or v <= 0:
                continue
            if self.waited[eng].get(s, 0) >= v:
                continue
            ws.append((s, v))
            self.waited[eng][s] = v
        marker = None
        if inc is not None:
            self.cnt[inc] += incn
            marker = (inc, self.cnt[inc])
        sems = self.sems

        nc_ = self.nc

        def fn(e, ws=tuple(ws), method=method, args=args, kw=kw, inc=inc,
               incn=incn, lowp=lowp):
            for s, v in ws:
                e.wait_ge(sems[s], v)
            if lowp:
                with nc_.allow_low_precision(reason="bf16 softmax-recip operand"):
                    ins = getattr(e, method)(*args, **kw)
            else:
                ins = getattr(e, method)(*args, **kw)
            if inc is not None:
                ins.then_inc(sems[inc], incn)

        self.ops[eng].append(fn)
        return marker

    def wait(self, eng, *waits):
        ws = []
        for wv in waits:
            if wv is None:
                continue
            s, v = wv
            if v is None or v <= 0:
                continue
            if self.waited[eng].get(s, 0) >= v:
                continue
            ws.append((s, v))
            self.waited[eng][s] = v
        if not ws:
            return
        sems = self.sems

        def fn(e, ws=tuple(ws)):
            for s, v in ws:
                e.wait_ge(sems[s], v)

        self.ops[eng].append(fn)

    def last(self, sem):
        return (sem, self.cnt[sem])


class PsumPool:

    def __init__(self, tensor_handle, nbanks):
        self.t = tensor_handle
        self.nbanks = nbanks
        self.readers = [[] for _ in range(nbanks)]
        self.ptr = 0

    def alloc(self, nb):
        if self.ptr + nb > self.nbanks:
            self.ptr = 0
        start = self.ptr
        self.ptr += nb
        waits = []
        for b in range(start, start + nb):
            waits.extend(self.readers[b])
            self.readers[b] = []
        ap = self.t[:, start * 512 : (start + nb) * 512]
        return ap, waits, (start, nb)

    def add_reader(self, slot, marker):
        start, nb = slot
        for b in range(start, start + nb):
            self.readers[b].append(marker)


def build_program(n_phases=4):
    nc = bass.Bass("TRN2", target_bir_lowering=False)

    xT = nc.dram_tensor("xT", [NDC, 128, S], F32, kind="ExternalInput").ap()
    wqa = nc.dram_tensor("wqa", [NDC, NDC, 128, 128], F32, kind="ExternalInput").ap()
    wka = nc.dram_tensor("wka", [NDC, NDC, 128, 128], F32, kind="ExternalInput").ap()
    wva = nc.dram_tensor("wva", [NDC, NDC, 128, 128], F32, kind="ExternalInput").ap()
    woa = nc.dram_tensor("woa", [NDC, NDC, 128, 128], F32, kind="ExternalInput").ap()
    biases = nc.dram_tensor("biases", [128, 64], F32, kind="ExternalInput").ap()
    U8 = mybir.dt.uint8
    wg = nc.dram_tensor("wg", [NHC, NDC, 128, 128], U8, kind="ExternalInput").ap()
    wu = nc.dram_tensor("wu", [NHC, NDC, 128, 128], U8, kind="ExternalInput").ap()
    wd = nc.dram_tensor("wd", [NDC, NHC, 128, 128], U8, kind="ExternalInput").ap()
    sgu = nc.dram_tensor("sgu", [128, 2 * NHC], F32, kind="ExternalInput").ap()
    sdn = nc.dram_tensor("sdn", [128, NDC], F32, kind="ExternalInput").ap()
    wqb = nc.dram_tensor("wqb", [NH, NDC, 128, 128], F32, kind="ExternalInput").ap()
    wkb = nc.dram_tensor("wkb", [NKV, NDC, 128, 128], F32, kind="ExternalInput").ap()
    wvb = nc.dram_tensor("wvb", [NKV, NDC, 128, 128], F32, kind="ExternalInput").ap()
    wob = nc.dram_tensor("wob", [NDC, NH, 128, 128], F32, kind="ExternalInput").ap()
    lnw = nc.dram_tensor("lnw", [128, NDC], F32, kind="ExternalInput").ap()
    ones_c = nc.dram_tensor("ones_c", [128, 128], BF16, kind="ExternalInput").ap()
    ident_c = nc.dram_tensor("ident_c", [128, 128], BF16, kind="ExternalInput").ap()
    neg128_c = nc.dram_tensor("neg128_c", [1, 128], BF16, kind="ExternalInput").ap()
    maskd_c = nc.dram_tensor("maskd_c", [128, 128], BF16, kind="ExternalInput").ap()
    maskw_c = nc.dram_tensor("maskw_c", [128, 128], BF16, kind="ExternalInput").ap()
    ident32_c = nc.dram_tensor("ident32_c", [128, 128], F32, kind="ExternalInput").ap()
    cos2_c = nc.dram_tensor("cos2_c", [128, S], BF16, kind="ExternalInput").ap()
    sin2_c = nc.dram_tensor("sin2_c", [128, S], BF16, kind="ExternalInput").ap()
    y = nc.dram_tensor("y", [NDC, 128, S], F32, kind="ExternalOutput").ap()
    if DBG["enable"]:
        ydbg = nc.dram_tensor("ydbg", [3, 8, 128, 640], F32, kind="ExternalOutput").ap()
        aodbg = nc.dram_tensor("aodbg", [128, NH * TSL], F32, kind="ExternalOutput").ap()

    st = ExitStack()
    E = st.enter_context
    em = Em(nc)
    for sname in ("dX", "dC", "oA", "oB", "pe", "dv", "ac", "gp",
                  "w0", "w1", "w2", "w3", "w4", "w5", "dD"):
        em.add_sem(sname, E(nc.semaphore(sname)))

    x_sb = E(nc.sbuf_tensor("x_sb", [128, NDC * S], BF16))
    ones_sb = E(nc.sbuf_tensor("ones_sb", [128, 128], BF16))
    ident_sb = E(nc.sbuf_tensor("ident_sb", [128, 128], BF16))
    neg128_sb = E(nc.sbuf_tensor("neg128_sb", [1, 128], BF16))
    b_sb = E(nc.sbuf_tensor("b_sb", [128, 64], F32))
    bqs_sb = E(nc.sbuf_tensor("bqs_sb", [128, 16], F32))
    sgu_sb = E(nc.sbuf_tensor("sgu_sb", [128, 2 * NHC], F32))
    sdn_sb = E(nc.sbuf_tensor("sdn_sb", [128, NDC], F32))
    lnw_sb = E(nc.sbuf_tensor("lnw_sb", [128, NDC], F32))
    mst_sb = E(nc.sbuf_tensor("mst_sb", [128, TSL], BF16))
    cst_sb = E(nc.sbuf_tensor("cst_sb", [128, TSL], BF16))

    def xc(dc, sl=None):
        if sl is None:
            return x_sb[:, dc * S : (dc + 1) * S]
        return x_sb[:, dc * S + sl * TSL : dc * S + sl * TSL + TSL]

    for dc in range(NDC):
        em.instr("gpsimd", "dma_start", x_sb[:, dc * S : (dc + 1) * S], xT[dc, :, :],
                 inc="dX", incn=16)
    x_loaded = em.last("dX")
    for dst, src in ((ones_sb, ones_c), (ident_sb, ident_c), (neg128_sb, neg128_c),
                     (b_sb, biases), (sgu_sb, sgu), (sdn_sb, sdn), (lnw_sb, lnw)):
        em.instr("sync", "dma_start", dst[:], src, inc="dC", incn=16)
    consts_loaded = em.last("dC")
    em.instr("vector", "tensor_scalar_mul", bqs_sb[:, :], b_sb[:, 0:16], SCALE_A,
             waits=[consts_loaded], inc="dv")
    constv_sb = E(nc.sbuf_tensor("constv_sb", [128, 4], F32))
    for j, v in enumerate((1e-5, 1e-11, LN_SCALE_A, -25.0)):
        em.instr("vector", "memset", constv_sb[:, j : j + 1], v, inc="dv")
        nc.const_aps.register(F32, v, constv_sb[:, j : j + 1])
    consts_ready = em.last("dv")
    em.wait("tensor", consts_loaded)
    em.wait("scalar", consts_ready)

    def ln_stats(sl, pool, kind, xwaits, xsq_sb, t_b, war_mark):
        ps_sum, w1, slot1 = pool.alloc(2)
        ps_sq, w2, slot2 = pool.alloc(2)
        sq_mk = [None] * NDC
        for dc in range(NDC):
            pp = xsq_sb[:, (dc % 2) * TSL : (dc % 2 + 1) * TSL]
            war = sq_mk[dc - 2] if dc >= 2 else None
            em.instr("vector", "tensor_mul", pp, xc(dc, sl), xc(dc, sl),
                     waits=list(xwaits) + [war], inc="dv")
            sq_ready = em.last("dv")
            for nh in range(2):
                cols = slice(nh * 512, nh * 512 + 512)
                em.instr("tensor", "matmul", ps_sum[:, cols], ones_sb[:, :],
                         xc(dc, sl)[:, cols], start=(dc == 0), stop=(dc == NDC - 1),
                         waits=list(xwaits) + (w1 if dc == 0 else []))
                sq_mk[dc] = em.instr("tensor", "matmul", ps_sq[:, cols], ones_sb[:, :],
                                     pp[:, cols], start=(dc == 0), stop=(dc == NDC - 1),
                                     waits=[sq_ready] + (w2 if dc == 0 else []),
                                     inc="pe")
        sums_done = em.last("pe")
        em.instr("vector", "tensor_scalar_mul", mst_sb[:], ps_sum[:, 0:TSL], 1.0 / D,
                 waits=[sums_done, war_mark], inc="dv")
        em.instr("vector", "tensor_scalar_mul", t_b[:], ps_sq[:, 0:TSL], 1.0 / D, inc="dv")
        em.instr("vector", "tensor_mul", cst_sb[:], mst_sb[:], mst_sb[:], inc="dv")
        em.instr("vector", "tensor_sub", t_b[:], t_b[:], cst_sb[:], inc="dv")
        var_done = em.last("dv")
        pool.add_reader(slot1, var_done)
        pool.add_reader(slot2, var_done)
        if kind == "attnb":
            a_, b_ = 1.000001, 1e-11
        else:
            a_, b_ = 1.0, 1e-5
        em.instr("scalar", "activation", t_b[:], t_b[:], AF.Ln, scale=a_, bias=b_,
                 waits=[var_done], inc="ac")
        em.instr("scalar", "activation", cst_sb[:], t_b[:], AF.Exp, scale=-0.5, inc="ac")
        return em.last("ac")

    def ln_apply(sl, xn_sb, kind, c_done, u_b):
        for dc in range(NDC):
            xn = xn_sb[:, dc * TSL : (dc + 1) * TSL]
            em.instr("vector", "tensor_sub", u_b[:], xc(dc, sl), mst_sb[:],
                     waits=[c_done], inc="dv")
            if kind == "attnb":
                em.instr("vector", "tensor_mul", u_b[:], u_b[:], cst_sb[:], inc="dv")
                em.instr("vector", "tensor_scalar_mul", xn, u_b[:],
                         lnw_sb[:, dc : dc + 1], inc="dv")
            else:
                em.instr("vector", "tensor_mul", xn, u_b[:], cst_sb[:], inc="dv")
        return em.last("dv")

    class WStream:

        def __init__(self, buf, nslots, blkcols, sems):
            self.buf = buf
            self.nslots = nslots
            self.blkcols = blkcols
            self.readers = [None] * nslots
            self.sems = sems
            self.i = 0

        def slot_ap(self, i):
            c = (i % self.nslots) * self.blkcols
            return self.buf[:, c : c + self.blkcols]

        def load(self, src_tiled, blk, nchunks):
            return self.load_ap(src_tiled[blk], nchunks)

        def load_ap(self, ap3d, nchunks):
            i = self.i
            self.i += 1
            sem = self.sems[i % self.nslots]
            em.instr("gpsimd", "dma_start",
                     self.slot_ap(i).rearrange("p (c q) -> p c q", c=nchunks),
                     ap3d.rearrange("c p q -> p c q"),
                     waits=[self.readers[i % self.nslots]], inc=sem, incn=16)
            return i, em.last(sem)

        def set_reader(self, i, marker):
            self.readers[i % self.nslots] = marker

    def emit_attn_a(next_kind):
        with ExitStack() as ph:
            P = ph.enter_context
            xn_sb = P(nc.sbuf_tensor("a_xn", [128, NDC * TSL], BF16))
            ctx_sb = P(nc.sbuf_tensor("a_ctx", [128, NDC * TSL], BF16))
            phiq_sb = P(nc.sbuf_tensor("a_phiq", [128, TSL], BF16))
            scrA = P(nc.sbuf_tensor("a_scrA", [128, TSL], F32))
            scrB = P(nc.sbuf_tensor("a_scrB", [128, TSL], F32))
            scrC = P(nc.sbuf_tensor("a_scrC", [128, TSL], F32))
            zeros_sb = P(nc.sbuf_tensor("a_zeros", [128, TSL], F32))
            carry_sb = P(nc.sbuf_tensor("a_carry", [128, NDC], F32))
            xsq_sb = P(nc.sbuf_tensor("a_xsq", [128, 2 * TSL], BF16))
            t_scr = P(nc.sbuf_tensor("a_st_t", [128, TSL], F32))
            u_scr = P(nc.sbuf_tensor("a_st_u", [128, TSL], BF16))
            wbuf = P(nc.sbuf_tensor("a_wbuf", [128, 6 * NDC * 128], BF16))
            psum = P(nc.psum_tensor("a_psum", [128, 8 * 512], F32))
            pool = PsumPool(psum, 8)
            ws = WStream(wbuf, 6, NDC * 128, ["w0", "w1", "w2", "w3", "w4", "w5"])

            em.instr("vector", "memset", zeros_sb[:], 0.0, waits=[consts_ready], inc="dv")
            em.instr("gpsimd", "memset", carry_sb[:], 0.0, inc="gp")
            carry_mark = [em.last("gp")] * NDC

            pending = {}

            def prefetch(sl, ec):
                for name, src in (("q", wqa), ("k", wka), ("v", wva)):
                    pending[(sl, ec, name)] = ws.load(src, ec, NDC)

            prefetch(0, 0)
            c_sl = ln_stats(0, pool, "plain", [x_loaded], xsq_sb, t_scr, None)
            c_next = None
            res_sl0 = None
            for sl in range(NSL):
                xn_ready = ln_apply(sl, xn_sb, "plain", c_sl, u_scr)
                if sl == 0:
                    c_sl = ln_stats(1, pool, "plain", [x_loaded], xsq_sb, t_scr,
                                    xn_ready)
                elif next_kind is not None:
                    c_next = ln_stats(0, pool, next_kind, [res_sl0], xsq_sb,
                                      t_scr, xn_ready)
                last_ctx = None
                for ec in range(NDC):
                    if ec + 1 < NDC:
                        prefetch(sl, ec + 1)
                    ps = {}
                    for name in ("q", "k", "v"):
                        i, loaded = pending.pop((sl, ec, name))
                        pp, wv, slot = pool.alloc(2)
                        for nh in range(2):
                            for dc in range(NDC):
                                mk = em.instr(
                                    "tensor", "matmul",
                                    pp[:, nh * 512 : nh * 512 + 512],
                                    ws.slot_ap(i)[:, dc * 128 : (dc + 1) * 128],
                                    xn_sb[:, dc * TSL + nh * 512 : dc * TSL + nh * 512 + 512],
                                    start=(dc == 0), stop=(dc == NDC - 1),
                                    waits=([loaded, xn_ready] + wv) if (nh == 0 and dc == 0) else (),
                                    inc="pe" if (nh == 1 and dc == NDC - 1) else None,
                                )
                        ws.set_reader(i, mk)
                        ps[name] = (pp, slot, mk)
                    ppq, slotq, mmq = ps["q"]
                    ppk, slotk, mmk = ps["k"]
                    ppv, slotv, mmv = ps["v"]
                    bq_ap = b_sb[:, ec : ec + 1]
                    bqs_ap = bqs_sb[:, ec : ec + 1]
                    bk_ap = b_sb[:, 16 + ec : 16 + ec + 1]
                    bv_ap = b_sb[:, 32 + ec : 32 + ec + 1]
                    em.instr("scalar", "activation", scrA[:], ppq[:, 0:TSL], AF.Relu,
                             scale=SCALE_A, bias=bqs_ap, waits=[mmq, em.last("dv")],
                             inc="ac")
                    rq_done = em.last("ac")
                    em.instr("vector", "tensor_scalar", scrB[:], ppq[:, 0:TSL],
                             bq_ap, 0.0, OP.add, OP.min, waits=[mmq, rq_done], inc="dv")
                    mnq_done = em.last("dv")
                    pool.add_reader(slotq, mnq_done)
                    em.instr("scalar", "activation", scrC[:], scrB[:], AF.Exp,
                             bias=LN_SCALE_A, waits=[mnq_done], inc="ac")
                    eq_done = em.last("ac")
                    em.instr("vector", "tensor_add", phiq_sb[:], scrA[:], scrC[:],
                             waits=[eq_done], inc="dv")
                    em.instr("scalar", "activation", scrA[:], ppk[:, 0:TSL], AF.Relu,
                             bias=bk_ap, waits=[mmk, em.last("dv")], inc="ac")
                    rk_done = em.last("ac")
                    em.instr("vector", "tensor_scalar", scrB[:], ppk[:, 0:TSL],
                             bk_ap, 0.0, OP.add, OP.min, waits=[mmk, rk_done], inc="dv")
                    mnk_done = em.last("dv")
                    pool.add_reader(slotk, mnk_done)
                    em.instr("scalar", "activation", scrC[:], scrB[:], AF.Exp,
                             waits=[mnk_done], inc="ac")
                    ek_done = em.last("ac")
                    em.instr("vector", "tensor_add", scrA[:], scrA[:], scrC[:],
                             waits=[ek_done], inc="dv")
                    em.instr("vector", "scalar_tensor_tensor", scrB[:], ppv[:, 0:TSL],
                             bv_ap, scrA[:], OP.add, OP.mult, waits=[mmv], inc="dv")
                    pool.add_reader(slotv, em.last("dv"))
                    em.instr("vector", "tensor_tensor_scan", scrC[:], scrB[:], zeros_sb[:],
                             carry_sb[:, ec : ec + 1], OP.add, OP.add,
                             waits=[carry_mark[ec]], inc="dv")
                    scan_done = em.last("dv")
                    if sl + 1 < NSL:
                        em.instr("scalar", "copy", carry_sb[:, ec : ec + 1],
                                 scrC[:, TSL - 1 : TSL], waits=[scan_done], inc="ac")
                        carry_mark[ec] = em.last("ac")
                    em.instr("vector", "tensor_mul",
                             ctx_sb[:, ec * TSL : (ec + 1) * TSL], phiq_sb[:], scrC[:],
                             inc="dv")
                    last_ctx = em.last("dv")
                wo_pending = {}
                wo_pending[0] = ws.load(woa, 0, NDC)
                wo_pending[1] = ws.load(woa, 1, NDC)
                for eo in range(NDC):
                    if eo + 2 < NDC:
                        wo_pending[eo + 2] = ws.load(woa, eo + 2, NDC)
                    elif eo == NDC - 2 and sl + 1 < NSL:
                        prefetch(sl + 1, 0)
                    i, loaded = wo_pending.pop(eo)
                    pp, wv, slot = pool.alloc(2)
                    for nh in range(2):
                        for ec in range(NDC):
                            mk = em.instr(
                                "tensor", "matmul",
                                pp[:, nh * 512 : nh * 512 + 512],
                                ws.slot_ap(i)[:, ec * 128 : (ec + 1) * 128],
                                ctx_sb[:, ec * TSL + nh * 512 : ec * TSL + nh * 512 + 512],
                                start=(ec == 0), stop=(ec == NDC - 1),
                                waits=([loaded, last_ctx] + wv) if (nh == 0 and ec == 0) else (),
                                inc="pe" if (nh == 1 and ec == NDC - 1) else None,
                            )
                    ws.set_reader(i, mk)
                    bo_ap = b_sb[:, 48 + eo : 48 + eo + 1]
                    em.instr("vector", "scalar_tensor_tensor", xc(eo, sl),
                             pp[:, 0:TSL], bo_ap, xc(eo, sl), OP.add, OP.add,
                             waits=[mk], inc="dv")
                    pool.add_reader(slot, em.last("dv"))
                if sl == 0:
                    res_sl0 = em.last("dv")
        return c_next, em.last("dv")

    def emit_ffn(c_sl, prev_res_sl1, final, tag, next_kind):
        with ExitStack() as ph:
            P = ph.enter_context
            psum = P(nc.psum_tensor(tag + "_psum", [128, 8 * 512], F32))
            pool = PsumPool(psum, 6)
            cs_ps = psum[:, 6 * 512 : 8 * 512]
            gu_done = None
            cs_reader = None
            stage_mk = [None, None]
            c_next = None
            res_sl0 = None
            for sl in range(NSL):
                with ExitStack() as slsc:
                    Q = slsc.enter_context
                    xnsc = ExitStack()
                    X_ = xnsc.enter_context
                    xn_sb = X_(nc.sbuf_tensor(f"{tag}{sl}_xn", [128, NDC * TSL], BF16))
                    with ExitStack() as lnsc:
                        L = lnsc.enter_context
                        t_scr = L(nc.sbuf_tensor(f"{tag}{sl}_st_t", [128, TSL], F32))
                        u_scr = L(nc.sbuf_tensor(f"{tag}{sl}_st_u", [128, TSL], BF16))
                        xsq_sb = L(nc.sbuf_tensor(f"{tag}{sl}_xsq", [128, 2 * TSL], BF16))
                        xn_ready = ln_apply(sl, xn_sb, "plain", c_sl, u_scr)
                        lnguard = xn_ready
                        if sl == 0:
                            c_sl = lnguard = ln_stats(1, pool, "plain",
                                                      [prev_res_sl1], xsq_sb,
                                                      t_scr, xn_ready)
                        elif next_kind is not None:
                            c_next = lnguard = ln_stats(0, pool, next_kind,
                                                        [res_sl0], xsq_sb,
                                                        t_scr, xn_ready)
                    act_sb = Q(nc.sbuf_tensor(f"{tag}{sl}_act", [128, NHC * TSL], BF16,
                                              side="right"))
                    with ExitStack() as gusc:
                        G = gusc.enter_context
                        gu_buf = G(nc.sbuf_tensor(f"{tag}{sl}_gu", [128, 6 * 8 * 128], BF16))
                        g_scr = G(nc.sbuf_tensor(f"{tag}{sl}_g", [128, 2 * TSL], BF16))
                        gws = WStream(gu_buf, 6, 8 * 128,
                                      ["w0", "w1", "w2", "w3", "w4", "w5"])
                        gws.readers = [lnguard] * 6
                        gu_jobs = []
                        for hc_ in range(NHC):
                            for src_ in (wg, wu):
                                for hf in range(2):
                                    gu_jobs.append(src_[hc_, hf * 8 : hf * 8 + 8])
                        gu_pend = {}
                        gu_next = [0]

                        def gu_load():
                            j = gu_next[0]
                            gu_next[0] += 1
                            gu_pend[j] = gws.load_ap(gu_jobs[j], 8)

                        for _ in range(6):
                            gu_load()
                        act_mk = {}
                        g_war = [None, None]

                        def emit_cs_act(hcl):
                            for nh in range(2):
                                em.instr("tensor", "matmul",
                                         cs_ps[:, nh * 512 : nh * 512 + 512],
                                         ones_sb[:, :],
                                         act_sb[:, hcl * TSL + nh * 512 :
                                                hcl * TSL + nh * 512 + 512],
                                         start=(hcl == 0), stop=(hcl == NHC - 1),
                                         waits=[act_mk[hcl], cs_reader]
                                         if hcl == 0 else [act_mk[hcl]],
                                         inc="pe" if (hcl == NHC - 1 and nh == 1) else None)

                        for hc in range(NHC):
                            pps = {}
                            for nmi, name in enumerate(("g", "u")):
                                j0 = (hc * 2 + nmi) * 2
                                i0, ld0 = gu_pend.pop(j0)
                                i1, ld1 = gu_pend.pop(j0 + 1)
                                pp, wv, slot = pool.alloc(2)
                                for nh in range(2):
                                    for dc in range(NDC):
                                        ii = i0 if dc < 8 else i1
                                        mk = em.instr(
                                            "tensor", "matmul",
                                            pp[:, nh * 512 : nh * 512 + 512],
                                            gws.slot_ap(ii)[:, (dc % 8) * 128 : (dc % 8 + 1) * 128],
                                            xn_sb[:, dc * TSL + nh * 512 : dc * TSL + nh * 512 + 512],
                                            start=(dc == 0), stop=(dc == NDC - 1),
                                            waits=([ld0, ld1, xn_ready] + wv)
                                            if (nh == 0 and dc == 0) else (),
                                            inc="pe" if (nh == 1 and dc == NDC - 1) else None,
                                        )
                                ws_mk = mk
                                gws.set_reader(i0, ws_mk)
                                gws.set_reader(i1, ws_mk)
                                pps[name] = (pp, slot, ws_mk)
                                for _ in range(2):
                                    if gu_next[0] < len(gu_jobs):
                                        gu_load()
                            ppg, slotg, mmg = pps["g"]
                            ppu, slotu, mmu = pps["u"]
                            gp = g_scr[:, (hc % 2) * TSL : (hc % 2 + 1) * TSL]
                            em.instr("scalar", "activation", gp, ppg[:, 0:TSL],
                                     AF.Silu, scale=sgu_sb[:, hc : hc + 1],
                                     waits=[mmg, g_war[hc % 2]], inc="ac")
                            g_done = em.last("ac")
                            pool.add_reader(slotg, g_done)
                            em.instr("vector", "scalar_tensor_tensor",
                                     act_sb[:, hc * TSL : (hc + 1) * TSL],
                                     gp, sgu_sb[:, NHC + hc : NHC + hc + 1],
                                     ppu[:, 0:TSL], OP.mult, OP.mult,
                                     waits=[mmu, g_done], inc="dv")
                            act_mk[hc] = g_war[hc % 2] = em.last("dv")
                            pool.add_reader(slotu, act_mk[hc])
                            if hc >= 2:
                                emit_cs_act(hc - 2)
                        for hcl in (NHC - 2, NHC - 1):
                            emit_cs_act(hcl)
                        gu_done = em.last("pe")
                    xnsc.close()
                    with ExitStack() as dnsc:
                        DN = dnsc.enter_context
                        dn_buf = DN(nc.sbuf_tensor(f"{tag}{sl}_dn", [128, 2 * NHC * 128], BF16))
                        ca_row = DN(nc.sbuf_tensor(f"{tag}{sl}_carow", [1, TSL], BF16))
                        if final:
                            stage = DN(nc.sbuf_tensor(f"{tag}{sl}_stage", [128, 2 * TSL], F32))
                        dws = WStream(dn_buf, 2, NHC * 128, ["w2", "w3"])
                        dws.readers = [gu_done, gu_done]
                        em.instr("vector", "tensor_copy", ca_row[0:1, :],
                                 cs_ps[0:1, 0:TSL], waits=[gu_done], inc="dv")
                        ca_done = em.last("dv")
                        cs_reader = ca_done
                        dn_pending = {0: dws.load(wd, 0, NHC)}
                        for dout in range(NDC):
                            if dout + 1 < NDC:
                                dn_pending[dout + 1] = dws.load(wd, dout + 1, NHC)
                            i, loaded = dn_pending.pop(dout)
                            pp, wv, slot = pool.alloc(2)
                            for nh in range(2):
                                for hc in range(NHC):
                                    em.instr(
                                        "tensor", "matmul",
                                        pp[:, nh * 512 : nh * 512 + 512],
                                        dws.slot_ap(i)[:, hc * 128 : (hc + 1) * 128],
                                        act_sb[:, hc * TSL + nh * 512 : hc * TSL + nh * 512 + 512],
                                        start=(hc == 0), stop=False,
                                        waits=([loaded] + wv) if (nh == 0 and hc == 0) else (),
                                    )
                                mk = em.instr(
                                    "tensor", "matmul",
                                    pp[:, nh * 512 : nh * 512 + 512],
                                    neg128_sb[0:1, :],
                                    ca_row[0:1, nh * 512 : nh * 512 + 512],
                                    start=False, stop=True,
                                    waits=[ca_done] if nh == 0 else (),
                                    inc="pe" if nh == 1 else None,
                                )
                            dws.set_reader(i, mk)
                            sd_ap = sdn_sb[:, dout : dout + 1]
                            if final:
                                sslot = stage[:, (dout % 2) * TSL : (dout % 2 + 1) * TSL]
                                em.instr("vector", "scalar_tensor_tensor", sslot,
                                         pp[:, 0:TSL], sd_ap, xc(dout, sl),
                                         OP.mult, OP.add,
                                         waits=[mk, stage_mk[dout % 2]], inc="dv")
                                pool.add_reader(slot, em.last("dv"))
                                osem = "oA" if dout % 2 == 0 else "oB"
                                em.instr("sync", "dma_start",
                                         y[dout, :, sl * TSL : (sl + 1) * TSL], sslot,
                                         waits=[em.last("dv")], inc=osem, incn=16)
                                stage_mk[dout % 2] = em.last(osem)
                            else:
                                em.instr("vector", "scalar_tensor_tensor",
                                         xc(dout, sl), pp[:, 0:TSL], sd_ap,
                                         xc(dout, sl), OP.mult, OP.add,
                                         waits=[mk], inc="dv")
                                pool.add_reader(slot, em.last("dv"))
                        if sl == 0:
                            res_sl0 = em.last("dv")
            if final:
                em.wait("sync", em.last("oA"), em.last("oB"))
            return c_next, em.last("dv")

    def emit_attn_b(c_sl, prev_res_sl1, next_kind):
        with ExitStack() as ph:
            P = ph.enter_context
            maskd_sb = P(nc.sbuf_tensor("b_maskd", [128, 128], BF16))
            maskw_sb = P(nc.sbuf_tensor("b_maskw", [128, 128], BF16))
            i32_sb = P(nc.sbuf_tensor("b_i32", [128, 128], F32))
            cos_sb = P(nc.sbuf_tensor("b_cos", [128, TSL], BF16))
            sin_sb = P(nc.sbuf_tensor("b_sin", [128, TSL], BF16))
            krot_sb = P(nc.sbuf_tensor("b_krot", [128, NKV * S], BF16))
            vtm_sb = P(nc.sbuf_tensor("b_vtm", [128, NKV * S], BF16))
            wsbuf = P(nc.sbuf_tensor("b_wsbuf", [128, 2 * NDC * 128], BF16))
            qrot_sb = P(nc.sbuf_tensor("b_qrot", [128, 4 * TSL], BF16))
            vbf_sb = P(nc.sbuf_tensor("b_vbf", [128, TSL], F32))
            u1_sb = P(nc.sbuf_tensor("b_u1", [128, TSL], BF16))
            u2_sb = P(nc.sbuf_tensor("b_u2", [128, TSL], BF16))
            u2s_sb = P(nc.sbuf_tensor("b_u2s", [128, TSL], BF16))
            attnT_sb = P(nc.sbuf_tensor("b_attnT", [128, 3 * 512], BF16))
            rden_sb = P(nc.sbuf_tensor("b_rden", [1, TSL], BF16))
            zrow_sb = P(nc.sbuf_tensor("b_zrow", [128, 512], BF16))
            bcast_sb = P(nc.sbuf_tensor("b_bcast", [128, TSL], BF16))
            arena = P(nc.psum_tensor("b_arena", [128, 8 * 512], F32))
            rotb = [arena[:, 0:512], arena[:, 512:1024]]
            proj_ps = arena[:, 2 * 512 : 4 * 512]
            av_ps = arena[:, 4 * 512 : 6 * 512]
            den_ps = arena[:, 6 * 512 : 8 * 512]

            em.instr("sync", "dma_start", maskd_sb[:], maskd_c, waits=[consts_ready],
                     inc="dC", incn=16)
            em.instr("sync", "dma_start", maskw_sb[:], maskw_c, inc="dC", incn=16)
            em.instr("sync", "dma_start", i32_sb[:], ident32_c, inc="dC", incn=16)
            masks_ok = em.last("dC")
            em.wait("vector", masks_ok)
            em.wait("tensor", masks_ok)

            em.instr("vector", "memset", zrow_sb[:], 0.0, inc="dv")
            zrow_ok = em.last("dv")
            ws = WStream(wsbuf, 2, NDC * 128, ["w0", "w1"])
            qr_mk = [None] * 4
            qr_war = [None] * 4
            kr_done = {}
            vtm_ok = {}
            vbf_war = None
            rope_done = None
            attnT_war = [None] * 3
            rot_war = [None, None]
            proj_war = None
            av_war = None
            den_war = None
            rden_war = None
            bcast_war = None
            ao_last = None
            c_next = None
            res_sl0 = None
            carry_readers = None

            for sl in range(NSL):
                qt_lo, qt_hi = sl * 8, sl * 8 + 7
                em.instr("sync", "dma_start", cos_sb[:],
                         cos2_c[:, sl * TSL : (sl + 1) * TSL],
                         waits=[rope_done or consts_ready], inc="dC", incn=16)
                em.instr("sync", "dma_start", sin_sb[:],
                         sin2_c[:, sl * TSL : (sl + 1) * TSL], inc="dC", incn=16)
                cs_ok = em.last("dC")
                jobs = []
                for kvh in range(NKV):
                    jobs.append((wkb, kvh))
                    jobs.append((wvb, kvh))
                for h in range(NH):
                    jobs.append((wqb, h))
                for eo in range(NDC):
                    jobs.append((wob, eo))
                pend = {}
                for j in range(2):
                    pend[j] = ws.load(jobs[j][0], jobs[j][1], NDC)
                nextload = [2]

                def consume(j):
                    return pend.pop(j)

                def after_use(i, mk):
                    ws.set_reader(i, mk)
                    if nextload[0] < len(jobs):
                        j = nextload[0]
                        nextload[0] += 1
                        pend[j] = ws.load(jobs[j][0], jobs[j][1], NDC)

                setup_pool = PsumPool(arena, 8)
                if carry_readers is not None:
                    setup_pool.readers = [list(r) for r in carry_readers]

                with ExitStack() as slsc:
                    Q = slsc.enter_context
                    xn_sb = Q(nc.sbuf_tensor(f"b{sl}_xn", [128, NDC * TSL], BF16))
                    lnsc = ExitStack()
                    L = lnsc.enter_context
                    t_scr = L(nc.sbuf_tensor(f"b{sl}_st_t", [128, TSL], F32))
                    u_scr = L(nc.sbuf_tensor(f"b{sl}_st_u", [128, TSL], BF16))
                    xsq_sb = L(nc.sbuf_tensor(f"b{sl}_xsq", [128, 2 * TSL], BF16))
                    xn_ready = ln_apply(sl, xn_sb, "attnb", c_sl, u_scr)

                    def proj_setup(jidx):
                        i, loaded = consume(jidx)
                        pp, wv, slot = setup_pool.alloc(2)
                        for nh in range(2):
                            for dc in range(NDC):
                                mk = em.instr(
                                    "tensor", "matmul",
                                    pp[:, nh * 512 : nh * 512 + 512],
                                    ws.slot_ap(i)[:, dc * 128 : (dc + 1) * 128],
                                    xn_sb[:, dc * TSL + nh * 512 : dc * TSL + nh * 512 + 512],
                                    start=(dc == 0), stop=(dc == NDC - 1),
                                    waits=([loaded, xn_ready] + wv)
                                    if (nh == 0 and dc == 0) else (),
                                    inc="pe" if (nh == 1 and dc == NDC - 1) else None,
                                )
                        return pp, slot, mk, i

                    def rope(pp, dst_ap, mm_mk, war=()):
                        em.instr("vector", "tensor_mul", u1_sb[:], pp[:, 0:TSL],
                                 cos_sb[:], waits=[mm_mk, cs_ok], inc="dv")
                        em.instr("vector", "tensor_mul", u2_sb[:], pp[:, 0:TSL],
                                 sin_sb[:], inc="dv")
                        em.instr("vector", "tensor_copy", u2s_sb[0:64, :],
                                 u2_sb[64:128, :], inc="dv")
                        em.instr("vector", "tensor_copy", u2s_sb[64:128, :],
                                 u2_sb[0:64, :], inc="dv")
                        em.instr("vector", "tensor_add", dst_ap[:, :],
                                 u1_sb[:], u2s_sb[:], waits=list(war), inc="dv")
                        return em.last("dv")

                    for kvh in range(NKV):
                        ppk, slotk, mmk, ik = proj_setup(2 * kvh)
                        kdst = krot_sb[:, kvh * S + sl * TSL : kvh * S + (sl + 1) * TSL]
                        kr = rope(ppk, kdst, mmk)
                        kr_done[(kvh, sl)] = kr
                        setup_pool.add_reader(slotk, kr)
                        after_use(ik, kr)
                        ppv, slotv, mmv, iv = proj_setup(2 * kvh + 1)
                        em.instr("scalar", "copy", vbf_sb[:], ppv[:, 0:TSL],
                                 waits=[mmv, vbf_war], inc="ac")
                        vb_ok = em.last("ac")
                        setup_pool.add_reader(slotv, vb_ok)
                        after_use(iv, vb_ok)
                        tps, twv, tslot = setup_pool.alloc(2)
                        for j in range(8):
                            tmk = em.instr("tensor", "transpose",
                                           tps[:, j * 128 : (j + 1) * 128],
                                           vbf_sb[:, j * 128 : (j + 1) * 128],
                                           i32_sb[:],
                                           waits=([vb_ok] + twv) if j == 0 else (),
                                           inc="pe" if j == 7 else None)
                        em.instr("scalar", "copy",
                                 vtm_sb[:, (kvh * NDC + sl * 8) * 128 :
                                        (kvh * NDC + sl * 8 + 8) * 128],
                                 tps[:, 0:1024], waits=[tmk], inc="ac")
                        vt_ok = em.last("ac")
                        setup_pool.add_reader(tslot, vt_ok)
                        vtm_ok[(kvh, sl)] = vt_ok
                        vbf_war = vt_ok

                    if sl == 0:
                        c_sl = ln_stats(1, setup_pool, "attnb", [prev_res_sl1],
                                        xsq_sb, t_scr, xn_ready)
                    elif next_kind is not None:
                        c_next = ln_stats(0, setup_pool, next_kind, [res_sl0],
                                          xsq_sb, t_scr, xn_ready)
                    lnsc.close()
                    attnout_sb = Q(nc.sbuf_tensor(f"b{sl}_ao", [128, NH * TSL], BF16,
                                                  side="right"))
                    sr = setup_pool.readers
                    rot_war = [sr[0][-1] if sr[0] else None,
                               sr[1][-1] if sr[1] else None]
                    proj_war = sr[3][-1] if sr[3] else (sr[2][-1] if sr[2] else None)
                    pw2 = sr[2][-1] if sr[2] else None
                    av_war = sr[5][-1] if sr[5] else (sr[4][-1] if sr[4] else None)
                    aw2 = sr[4][-1] if sr[4] else None
                    den_war = sr[7][-1] if sr[7] else (sr[6][-1] if sr[6] else None)
                    dw2 = sr[6][-1] if sr[6] else None

                    qpend = {}
                    rot_i = [0]
                    buf_i = [0]

                    def qproj_start(h):
                        qpend[h] = consume(8 + h)

                    def qproj_gen(h, extra=()):
                        i, loaded = qpend[h]
                        for part in range(4):
                            nh, half = part // 2, part % 2
                            for dc in range(half * 8, half * 8 + 8):
                                mk = em.instr(
                                    "tensor", "matmul",
                                    proj_ps[:, nh * 512 : nh * 512 + 512],
                                    ws.slot_ap(i)[:, dc * 128 : (dc + 1) * 128],
                                    xn_sb[:, dc * TSL + nh * 512 : dc * TSL + nh * 512 + 512],
                                    start=(dc == 0), stop=(dc == NDC - 1),
                                    waits=([loaded, xn_ready, proj_war] + list(extra))
                                    if (part == 0 and dc == 0) else (),
                                    inc="pe" if part == 3 and dc == NDC - 1 else None,
                                )
                                if not (part == 3 and dc == NDC - 1):
                                    yield None
                        qs = h % 4
                        qr_mk[qs] = rope(proj_ps, qrot_sb[:, qs * TSL : (qs + 1) * TSL],
                                         mk, war=[qr_war[qs]])
                        after_use(i, qr_mk[qs])
                        yield qr_mk[qs]

                    def qproj_part(h, part, extra=()):
                        g = qproj_gen(h, extra)
                        r = None
                        for r in g:
                            pass
                        return r

                    qproj_start(0)
                    proj_war = qproj_part(0, 0, extra=[pw2] if pw2 else [])
                    qproj_start(1)
                    proj_war = qproj_part(1, 0)

                    pending_finish = [None]
                    for h in range(NH):
                        kvh = h // 4
                        qs = h % 4
                        chunks = []
                        for kt in range(max(0, qt_lo - 4), qt_hi + 1):
                            qlo_t = max(kt, qt_lo)
                            qhi_t = min(kt + 4, qt_hi)
                            if qhi_t - qlo_t + 1 == 5:
                                chunks.append((kt, qlo_t, qlo_t + 3))
                                chunks.append((kt, qhi_t, qhi_t))
                            else:
                                chunks.append((kt, qlo_t, qhi_t))
                        nxt = h + 2
                        proj_g = None
                        per_chunk = 0
                        if nxt < NH:
                            qproj_start(nxt)
                            proj_g = qproj_gen(nxt)
                            per_chunk = max(1, (33 + len(chunks) - 2) // max(1, len(chunks) - 1))
                        pending = []
                        den_mk = av_mk = None

                        def emit_denav(pc):
                            nonlocal den_mk, av_mk
                            kt, qlo, qhi, rb, buf, t_waits = pc
                            sl_kt = 0 if kt < 8 else 1
                            strip = attnT_sb[:, buf * 512 : buf * 512 + (qhi - qlo + 1) * 128]
                            groups = []
                            for qtg in range(qlo, qhi + 1):
                                bank = 0 if qtg < qt_lo + 4 else 1
                                if groups and groups[-1][2] == bank:
                                    groups[-1][1] = qtg
                                else:
                                    groups.append([qtg, qtg, bank])
                            for g_lo, g_hi, g_bank in groups:
                                sp = (g_bank == 0 and kt == qt_lo + 3) or \
                                     (g_bank == 1 and kt == qt_hi)
                                oc = slice((g_lo - qt_lo) * 128, (g_hi + 1 - qt_lo) * 128)
                                mc = slice((g_lo - qlo) * 128, (g_hi + 1 - qlo) * 128)
                                den_mk = em.instr(
                                    "tensor", "matmul", den_ps[0:1, oc],
                                    ones_sb[:, 0:1], strip[:, mc],
                                    start=False, stop=sp, waits=list(t_waits),
                                    inc="pe")
                                av_mk = em.instr(
                                    "tensor", "matmul", av_ps[:, oc],
                                    vtm_sb[:, (kvh * NDC + kt) * 128 :
                                           (kvh * NDC + kt + 1) * 128],
                                    strip[:, mc],
                                    start=False, stop=sp,
                                    waits=[vtm_ok[(kvh, sl_kt)]], inc="pe")
                            attnT_war[buf] = av_mk

                        for ci, (kt, qlo, qhi) in enumerate(chunks):
                            nqc = qhi - qlo + 1
                            sl_kt = 0 if kt < 8 else 1
                            rb = rot_i[0] % 2
                            rot_i[0] += 1
                            buf_i[0] += 1
                            buf = buf_i[0] % 3
                            smk = em.instr(
                                "tensor", "matmul", rotb[rb][:, 0 : nqc * 128],
                                krot_sb[:, kvh * S + kt * 128 : kvh * S + (kt + 1) * 128],
                                qrot_sb[:, qs * TSL + (qlo - qt_lo) * 128 :
                                        qs * TSL + (qhi + 1 - qt_lo) * 128],
                                start=True, stop=True,
                                waits=[kr_done[(kvh, sl_kt)], qr_mk[qs], rot_war[rb]],
                                inc="pe")
                            qr_war[qs] = smk
                            strip = attnT_sb[:, buf * 512 : buf * 512 + nqc * 128]
                            em.instr("scalar", "activation", strip,
                                     rotb[rb][:, 0 : nqc * 128], AF.Exp,
                                     scale=SCALE_A, bias=-25.0,
                                     waits=[smk, attnT_war[buf]], inc="ac")
                            e_mk = em.last("ac")
                            rot_war[rb] = e_mk
                            t_waits = [e_mk]
                            if qlo == kt:
                                em.instr("vector", "tensor_mul",
                                         attnT_sb[:, buf * 512 : buf * 512 + 128],
                                         attnT_sb[:, buf * 512 : buf * 512 + 128],
                                         maskd_sb[:], waits=[e_mk], inc="dv")
                                t_waits.append(em.last("dv"))
                            if qhi == kt + 4:
                                em.instr("vector", "tensor_mul",
                                         attnT_sb[:, buf * 512 + (nqc - 1) * 128 :
                                                  buf * 512 + nqc * 128],
                                         attnT_sb[:, buf * 512 + (nqc - 1) * 128 :
                                                  buf * 512 + nqc * 128],
                                         maskw_sb[:], waits=[e_mk], inc="dv")
                                t_waits.append(em.last("dv"))
                            if ci == 0:
                                if pending_finish[0] is not None:
                                    pending_finish[0]()
                                    pending_finish[0] = None
                                ow = [zrow_ok, den_war, dw2]
                                for bk in range(2):
                                    em.instr("tensor", "matmul",
                                             den_ps[0:1, bk * 512 : bk * 512 + 512],
                                             ones_sb[0:1, 0:1], zrow_sb[0:1, :],
                                             start=True, stop=False,
                                             waits=ow if bk == 0 else (), inc="pe")
                                ow = [av_war, aw2]
                                for bk in range(2):
                                    em.instr("tensor", "matmul",
                                             av_ps[:, bk * 512 : bk * 512 + 512],
                                             ones_sb[:, :], zrow_sb[:, :],
                                             start=True, stop=False,
                                             waits=ow if bk == 0 else (), inc="pe")
                                dw2 = aw2 = None
                            if proj_g is not None and ci > 0:
                                for _ in range(per_chunk):
                                    mkq = next(proj_g, StopIteration)
                                    if mkq is StopIteration:
                                        proj_g = None
                                        break
                                    if mkq is not None:
                                        proj_war = mkq
                            if len(pending) >= 1:
                                emit_denav(pending.pop(0))
                            pending.append((kt, qlo, qhi, rb, buf, t_waits))
                        while pending:
                            emit_denav(pending.pop(0))
                        if proj_g is not None:
                            for mkq in proj_g:
                                if mkq is not None:
                                    proj_war = mkq
                            proj_g = None
                        em.instr("vector", "reciprocal", rden_sb[0:1, :],
                                 den_ps[0:1, 0:TSL],
                                 waits=[den_mk, rden_war], inc="dv", lowp=True)
                        rc_mk = em.last("dv")
                        den_war = rc_mk

                        def finish(h=h, av_mk=av_mk, rc_mk=rc_mk):
                            nonlocal rden_war, bcast_war, av_war, ao_last
                            bmks = []
                            for nh2 in range(2):
                                rb = rot_i[0] % 2
                                rot_i[0] += 1
                                bmk = em.instr(
                                    "tensor", "matmul", rotb[rb][:, 0:512],
                                    ones_sb[0:1, :],
                                    rden_sb[0:1, nh2 * 512 : nh2 * 512 + 512],
                                    start=True, stop=True,
                                    waits=[rc_mk, rot_war[rb]],
                                    inc="pe")
                                rden_war = bmk
                                em.instr("vector", "tensor_copy",
                                         bcast_sb[:, nh2 * 512 : nh2 * 512 + 512],
                                         rotb[rb][:, 0:512],
                                         waits=[bmk, bcast_war], inc="dv")
                                rot_war[rb] = em.last("dv")
                                bmks.append(em.last("dv"))
                            em.instr("vector", "tensor_mul",
                                     attnout_sb[:, h * TSL : (h + 1) * TSL],
                                     av_ps[:, 0:TSL], bcast_sb[:],
                                     waits=[av_mk] + bmks, inc="dv")
                            ao_last = av_war = bcast_war = em.last("dv")

                        pending_finish[0] = finish
                    pending_finish[0]()
                    pending_finish[0] = None

                    wo_pool = PsumPool(arena, 4)
                    wo_pool.readers = [[rot_war[0]], [rot_war[1]],
                                       [proj_war], [proj_war]]
                    for eo in range(NDC):
                        i, loaded = consume(24 + eo)
                        pp, wv, slot = wo_pool.alloc(2)
                        for nh in range(2):
                            for ei in range(NH):
                                mk = em.instr(
                                    "tensor", "matmul",
                                    pp[:, nh * 512 : nh * 512 + 512],
                                    ws.slot_ap(i)[:, ei * 128 : (ei + 1) * 128],
                                    attnout_sb[:, ei * TSL + nh * 512 :
                                               ei * TSL + nh * 512 + 512],
                                    start=(ei == 0), stop=(ei == NH - 1),
                                    waits=([loaded, ao_last] + wv)
                                    if (nh == 0 and ei == 0) else (),
                                    inc="pe" if (nh == 1 and ei == NH - 1) else None,
                                )
                        after_use(i, mk)
                        em.instr("vector", "tensor_add", xc(eo, sl), xc(eo, sl),
                                 pp[:, 0:TSL], waits=[mk], inc="dv")
                        wo_pool.add_reader(slot, em.last("dv"))
                    if sl == 0:
                        res_sl0 = em.last("dv")
                    rope_done = em.last("dv")
                    wr = wo_pool.readers
                    carry_readers = [wr[0], wr[1], wr[2], wr[3],
                                     [av_war], [av_war], [den_war], [den_war]]
        return c_next, em.last("dv")

    def emit_flush(x_ready):
        with ExitStack() as ph:
            P = ph.enter_context
            stage = P(nc.sbuf_tensor("fl_stage", [128, 2 * TSL], F32))
            dma_mk = [None] * 2
            k = 0
            for dc in range(NDC):
                for sl in range(NSL):
                    sslot = stage[:, (k % 2) * TSL : (k % 2 + 1) * TSL]
                    em.instr("vector", "tensor_copy", sslot, xc(dc, sl),
                             waits=[x_ready, dma_mk[k % 2]], inc="dv")
                    osem = "oA" if k % 2 == 0 else "oB"
                    em.instr("sync", "dma_start",
                             y[dc, :, sl * TSL : (sl + 1) * TSL], sslot,
                             waits=[em.last("dv")], inc=osem, incn=16)
                    dma_mk[k % 2] = em.last(osem)
                    k += 1
            em.wait("sync", em.last("oA"), em.last("oB"))

    def phase_gate():
        snap = [em.last(s) for s in ("pe", "dv", "ac")]
        for e in ("gpsimd", "sync", "tensor", "vector", "scalar"):
            em.wait(e, *snap)

    c, res1 = emit_attn_a("plain" if n_phases >= 2 else None)
    if n_phases >= 2:
        phase_gate()
        c, res1 = emit_ffn(c, res1, final=False, tag="f1",
                           next_kind="attnb" if n_phases >= 3 else None)
    if n_phases >= 3:
        phase_gate()
        c, res1 = emit_attn_b(c, res1,
                              next_kind="plain" if n_phases >= 4 else None)
    if n_phases >= 4:
        phase_gate()
        emit_ffn(c, res1, final=True, tag="f2", next_kind=None)
    else:
        phase_gate()
        emit_flush(res1)

    with nc.Block() as block:
        @block.sync
        def _(e):
            for fn in em.ops["sync"]:
                fn(e)

        @block.gpsimd
        def _(e):
            for fn in em.ops["gpsimd"]:
                fn(e)

        @block.tensor
        def _(e):
            for fn in em.ops["tensor"]:
                fn(e)

        @block.vector
        def _(e):
            for fn in em.ops["vector"]:
                fn(e)

        @block.scalar
        def _(e):
            for fn in em.ops["scalar"]:
                fn(e)

    st.close()
    return nc


def _prep_shared(ins):
    f32 = np.float32
    out = {}
    t44 = lambda a, r, c: np.ascontiguousarray(
        a.reshape(r, 128, c, 128).transpose(2, 0, 1, 3))
    out["wqa"] = t44(np.ascontiguousarray(ins["L0_attn_a_Wq"].T), NDC, NDC)
    out["wka"] = t44(np.ascontiguousarray(ins["L0_attn_a_Wk"].T), NDC, NDC)
    out["wva"] = t44(np.ascontiguousarray(ins["L0_attn_a_Wv"].T), NDC, NDC)
    out["woa"] = t44(np.ascontiguousarray(ins["L0_attn_a_Wo"].T), NDC, NDC)
    cols = lambda v, n: np.ascontiguousarray(v.reshape(n, 128).T, dtype=f32)
    out["biases"] = np.concatenate(
        [cols(ins["L0_attn_a_b" + k], 16) for k in "qkvo"], axis=1)
    out["wg"] = t44(np.ascontiguousarray(ins["L0_ffn_wq_gate"].T), NDC, NHC).astype(np.uint8)
    out["wu"] = t44(np.ascontiguousarray(ins["L0_ffn_wq_up"].T), NDC, NHC).astype(np.uint8)
    out["wd"] = t44(np.ascontiguousarray(ins["L0_ffn_wq_down"].T), NHC, NDC).astype(np.uint8)
    out["sgu"] = np.concatenate(
        [cols(ins["L0_ffn_s_gate"], NHC), cols(ins["L0_ffn_s_up"], NHC)], axis=1)
    out["sdn"] = cols(ins["L0_ffn_s_down"], NDC)
    perm = np.concatenate([np.arange(0, 128, 2), np.arange(1, 128, 2)])
    qp = ins["L0_attn_b_wq"].reshape(D, NH, 128)[:, :, perm]
    kp = ins["L0_attn_b_wk"].reshape(D, NKV, 128)[:, :, perm]
    vp = ins["L0_attn_b_wv"].reshape(D, NKV, 128)
    out["wqb"] = np.ascontiguousarray(
        qp.transpose(1, 0, 2).reshape(NH, NDC, 128, 128))
    out["wkb"] = np.ascontiguousarray(
        kp.transpose(1, 0, 2).reshape(NKV, NDC, 128, 128))
    out["wvb"] = np.ascontiguousarray(
        vp.transpose(1, 0, 2).reshape(NKV, NDC, 128, 128))
    out["wob"] = t44(ins["L0_attn_b_wo"], NDC, NH)
    out["lnw"] = cols(ins["L0_attn_b_ln_w"], NDC)
    out["ones_c"] = np.ones((128, 128), dtype=ml_dtypes.bfloat16)
    out["ident_c"] = np.eye(128, dtype=ml_dtypes.bfloat16)
    out["neg128_c"] = np.full((1, 128), -128.0, dtype=ml_dtypes.bfloat16)
    kk = np.arange(128)[:, None]
    qq = np.arange(128)[None, :]
    out["maskd_c"] = (kk <= qq).astype(ml_dtypes.bfloat16)
    out["maskw_c"] = (kk >= qq).astype(ml_dtypes.bfloat16)
    out["ident32_c"] = np.eye(128, dtype=f32)
    invf = 1.0 / 10000.0 ** (np.arange(0, HD, 2, dtype=np.float64) / HD)
    ang = np.arange(S, dtype=np.float64)[:, None] * invf[None, :]
    cosT = np.cos(ang).T.astype(ml_dtypes.bfloat16)
    sinT = np.sin(ang).T.astype(ml_dtypes.bfloat16)
    out["cos2_c"] = np.ascontiguousarray(np.concatenate([cosT, cosT], axis=0))
    out["sin2_c"] = np.ascontiguousarray(np.concatenate([sinT, -sinT], axis=0))
    return out


def run(inputs, n_phases=4, trace=False):
    shared = _prep_shared(inputs)
    x = np.asarray(inputs["L0_attn_a_x"], dtype=np.float32)
    B = x.shape[0]
    in_maps = []
    for b in range(B):
        m = dict(shared)
        m["xT"] = np.ascontiguousarray(x[b].T).reshape(NDC, 128, S)
        in_maps.append(m)
    nc = build_program(n_phases=n_phases)
    res = run_bass_kernel_spmd(nc, in_maps, core_ids=list(range(8)), trace=trace)
    outs = []
    for b in range(B):
        yb = res.results[b]["y"]
        outs.append(yb.reshape(D, S).T)
    return np.stack(outs, axis=0), res


def kernel(**inputs):
    out, _ = run(inputs, n_phases=4)
    return out.astype(np.float32)


# version 23
# speedup vs baseline: 1.1968x; 1.0112x over previous
import sys
from contextlib import ExitStack

import numpy as np
import ml_dtypes

sys.path.insert(0, "/opt/trn_rl_repo")
import concourse.bass as bass
import concourse.mybir as mybir
from concourse.bass_utils import run_bass_kernel_spmd
from concourse.alu_op_type import AluOpType as OP

F32 = mybir.dt.float32
BF16 = mybir.dt.bfloat16
I32 = mybir.dt.int32
AF = mybir.ActivationFunctionType
AX = mybir.AxisListType

D = 2048
S = 2048
NDC = D // 128
TSL = 1024
NSL = S // TSL
HID = 5632
NHC = HID // 128
NH = 16
NKV = 4
HD = 128
WIN = 512
SCALE_A = 128.0 ** -0.5
LN_SCALE_A = float(np.log(SCALE_A))

ENGINES = ("sync", "gpsimd", "tensor", "vector", "scalar")

DBG = {"enable": False}


class Em:

    def __init__(self, nc):
        self.nc = nc
        self.ops = {e: [] for e in ENGINES}
        self.sems = {}
        self.cnt = {}
        self.waited = {e: {} for e in ENGINES}

    def add_sem(self, name, handle):
        self.sems[name] = handle
        self.cnt[name] = 0

    def instr(self, eng, method, *args, waits=(), inc=None, incn=1, lowp=False, **kw):
        ws = []
        for wv in waits:
            if wv is None:
                continue
            s, v = wv
            if v is None or v <= 0:
                continue
            if self.waited[eng].get(s, 0) >= v:
                continue
            ws.append((s, v))
            self.waited[eng][s] = v
        marker = None
        if inc is not None:
            self.cnt[inc] += incn
            marker = (inc, self.cnt[inc])
        sems = self.sems

        nc_ = self.nc

        def fn(e, ws=tuple(ws), method=method, args=args, kw=kw, inc=inc,
               incn=incn, lowp=lowp):
            for s, v in ws:
                e.wait_ge(sems[s], v)
            if lowp:
                with nc_.allow_low_precision(reason="bf16 softmax-recip operand"):
                    ins = getattr(e, method)(*args, **kw)
            else:
                ins = getattr(e, method)(*args, **kw)
            if inc is not None:
                ins.then_inc(sems[inc], incn)

        self.ops[eng].append(fn)
        return marker

    def wait(self, eng, *waits):
        ws = []
        for wv in waits:
            if wv is None:
                continue
            s, v = wv
            if v is None or v <= 0:
                continue
            if self.waited[eng].get(s, 0) >= v:
                continue
            ws.append((s, v))
            self.waited[eng][s] = v
        if not ws:
            return
        sems = self.sems

        def fn(e, ws=tuple(ws)):
            for s, v in ws:
                e.wait_ge(sems[s], v)

        self.ops[eng].append(fn)

    def last(self, sem):
        return (sem, self.cnt[sem])


class PsumPool:

    def __init__(self, tensor_handle, nbanks):
        self.t = tensor_handle
        self.nbanks = nbanks
        self.readers = [[] for _ in range(nbanks)]
        self.ptr = 0

    def alloc(self, nb):
        if self.ptr + nb > self.nbanks:
            self.ptr = 0
        start = self.ptr
        self.ptr += nb
        waits = []
        for b in range(start, start + nb):
            waits.extend(self.readers[b])
            self.readers[b] = []
        ap = self.t[:, start * 512 : (start + nb) * 512]
        return ap, waits, (start, nb)

    def add_reader(self, slot, marker):
        start, nb = slot
        for b in range(start, start + nb):
            self.readers[b].append(marker)


def build_program(n_phases=4):
    nc = bass.Bass("TRN2", target_bir_lowering=False)

    xT = nc.dram_tensor("xT", [NDC, 128, S], F32, kind="ExternalInput").ap()
    wqa = nc.dram_tensor("wqa", [NDC, 128, NDC * 128], F32, kind="ExternalInput").ap()
    wka = nc.dram_tensor("wka", [NDC, 128, NDC * 128], F32, kind="ExternalInput").ap()
    wva = nc.dram_tensor("wva", [NDC, 128, NDC * 128], F32, kind="ExternalInput").ap()
    woa = nc.dram_tensor("woa", [NDC, 128, NDC * 128], F32, kind="ExternalInput").ap()
    biases = nc.dram_tensor("biases", [128, 64], F32, kind="ExternalInput").ap()
    U8 = mybir.dt.uint8
    wg = nc.dram_tensor("wg", [NHC, 2, 128, 8 * 128], U8, kind="ExternalInput").ap()
    wu = nc.dram_tensor("wu", [NHC, 2, 128, 8 * 128], U8, kind="ExternalInput").ap()
    wd = nc.dram_tensor("wd", [NDC, 128, NHC * 128], U8, kind="ExternalInput").ap()
    sgu = nc.dram_tensor("sgu", [128, 2 * NHC], F32, kind="ExternalInput").ap()
    sdn = nc.dram_tensor("sdn", [128, NDC], F32, kind="ExternalInput").ap()
    wqb = nc.dram_tensor("wqb", [NH, 128, NDC * 128], F32, kind="ExternalInput").ap()
    wkb = nc.dram_tensor("wkb", [NKV, 128, NDC * 128], F32, kind="ExternalInput").ap()
    wvb = nc.dram_tensor("wvb", [NKV, 128, NDC * 128], F32, kind="ExternalInput").ap()
    wob = nc.dram_tensor("wob", [NDC, 128, NH * 128], F32, kind="ExternalInput").ap()
    lnw = nc.dram_tensor("lnw", [128, NDC], F32, kind="ExternalInput").ap()
    ones_c = nc.dram_tensor("ones_c", [128, 128], BF16, kind="ExternalInput").ap()
    ident_c = nc.dram_tensor("ident_c", [128, 128], BF16, kind="ExternalInput").ap()
    neg128_c = nc.dram_tensor("neg128_c", [1, 128], BF16, kind="ExternalInput").ap()
    maskd_c = nc.dram_tensor("maskd_c", [128, 128], BF16, kind="ExternalInput").ap()
    maskw_c = nc.dram_tensor("maskw_c", [128, 128], BF16, kind="ExternalInput").ap()
    ident32_c = nc.dram_tensor("ident32_c", [128, 128], F32, kind="ExternalInput").ap()
    cos2_c = nc.dram_tensor("cos2_c", [128, S], BF16, kind="ExternalInput").ap()
    sin2_c = nc.dram_tensor("sin2_c", [128, S], BF16, kind="ExternalInput").ap()
    y = nc.dram_tensor("y", [NDC, 128, S], F32, kind="ExternalOutput").ap()
    if DBG["enable"]:
        ydbg = nc.dram_tensor("ydbg", [3, 8, 128, 640], F32, kind="ExternalOutput").ap()
        aodbg = nc.dram_tensor("aodbg", [128, NH * TSL], F32, kind="ExternalOutput").ap()

    st = ExitStack()
    E = st.enter_context
    em = Em(nc)
    for sname in ("dX", "dC", "oA", "oB", "pe", "dv", "ac", "gp",
                  "w0", "w1", "w2", "w3", "w4", "w5", "dD"):
        em.add_sem(sname, E(nc.semaphore(sname)))

    x_sb = E(nc.sbuf_tensor("x_sb", [128, NDC * S], BF16))
    ones_sb = E(nc.sbuf_tensor("ones_sb", [128, 128], BF16))
    ident_sb = E(nc.sbuf_tensor("ident_sb", [128, 128], BF16))
    neg128_sb = E(nc.sbuf_tensor("neg128_sb", [1, 128], BF16))
    b_sb = E(nc.sbuf_tensor("b_sb", [128, 64], F32))
    bqs_sb = E(nc.sbuf_tensor("bqs_sb", [128, 16], F32))
    sgu_sb = E(nc.sbuf_tensor("sgu_sb", [128, 2 * NHC], F32))
    sdn_sb = E(nc.sbuf_tensor("sdn_sb", [128, NDC], F32))
    lnw_sb = E(nc.sbuf_tensor("lnw_sb", [128, NDC], F32))
    mst_sb = E(nc.sbuf_tensor("mst_sb", [128, TSL], BF16))
    cst_sb = E(nc.sbuf_tensor("cst_sb", [128, TSL], BF16))

    def xc(dc, sl=None):
        if sl is None:
            return x_sb[:, dc * S : (dc + 1) * S]
        return x_sb[:, dc * S + sl * TSL : dc * S + sl * TSL + TSL]

    for dc in range(NDC):
        em.instr("gpsimd", "dma_start", x_sb[:, dc * S : dc * S + TSL],
                 xT[dc, :, 0:TSL], inc="dX", incn=16)
    x_sl0 = em.last("dX")
    for dc in range(NDC):
        em.instr("gpsimd", "dma_start", x_sb[:, dc * S + TSL : (dc + 1) * S],
                 xT[dc, :, TSL:S], inc="dX", incn=16)
    x_loaded = em.last("dX")
    for dst, src in ((ones_sb, ones_c), (ident_sb, ident_c), (neg128_sb, neg128_c),
                     (b_sb, biases), (sgu_sb, sgu), (sdn_sb, sdn), (lnw_sb, lnw)):
        em.instr("sync", "dma_start", dst[:], src, inc="dC", incn=16)
    consts_loaded = em.last("dC")
    em.instr("vector", "tensor_scalar_mul", bqs_sb[:, :], b_sb[:, 0:16], SCALE_A,
             waits=[consts_loaded], inc="dv")
    constv_sb = E(nc.sbuf_tensor("constv_sb", [128, 4], F32))
    for j, v in enumerate((1e-5, 1e-11, LN_SCALE_A, -25.0)):
        em.instr("vector", "memset", constv_sb[:, j : j + 1], v, inc="dv")
        nc.const_aps.register(F32, v, constv_sb[:, j : j + 1])
    consts_ready = em.last("dv")
    em.wait("tensor", consts_loaded)
    em.wait("scalar", consts_ready)

    def ln_stats(sl, pool, kind, xwaits, xsq_sb, t_b, war_mark):
        ps_sum, w1, slot1 = pool.alloc(2)
        ps_sq, w2, slot2 = pool.alloc(2)
        sq_mk = [None] * NDC
        for dc in range(NDC):
            pp = xsq_sb[:, (dc % 2) * TSL : (dc % 2 + 1) * TSL]
            war = sq_mk[dc - 2] if dc >= 2 else None
            em.instr("vector", "tensor_mul", pp, xc(dc, sl), xc(dc, sl),
                     waits=list(xwaits) + [war], inc="dv")
            sq_ready = em.last("dv")
            for nh in range(2):
                cols = slice(nh * 512, nh * 512 + 512)
                em.instr("tensor", "matmul", ps_sum[:, cols], ones_sb[:, :],
                         xc(dc, sl)[:, cols], start=(dc == 0), stop=(dc == NDC - 1),
                         waits=list(xwaits) + (w1 if dc == 0 else []))
                sq_mk[dc] = em.instr("tensor", "matmul", ps_sq[:, cols], ones_sb[:, :],
                                     pp[:, cols], start=(dc == 0), stop=(dc == NDC - 1),
                                     waits=[sq_ready] + (w2 if dc == 0 else []),
                                     inc="pe")
        sums_done = em.last("pe")
        em.instr("vector", "tensor_scalar_mul", mst_sb[:], ps_sum[:, 0:TSL], 1.0 / D,
                 waits=[sums_done, war_mark], inc="dv")
        em.instr("vector", "tensor_scalar_mul", t_b[:], ps_sq[:, 0:TSL], 1.0 / D, inc="dv")
        em.instr("vector", "tensor_mul", cst_sb[:], mst_sb[:], mst_sb[:], inc="dv")
        em.instr("vector", "tensor_sub", t_b[:], t_b[:], cst_sb[:], inc="dv")
        var_done = em.last("dv")
        pool.add_reader(slot1, var_done)
        pool.add_reader(slot2, var_done)
        if kind == "attnb":
            a_, b_ = 1.000001, 1e-11
        else:
            a_, b_ = 1.0, 1e-5
        em.instr("scalar", "activation", t_b[:], t_b[:], AF.Ln, scale=a_, bias=b_,
                 waits=[var_done], inc="ac")
        em.instr("scalar", "activation", cst_sb[:], t_b[:], AF.Exp, scale=-0.5, inc="ac")
        return em.last("ac")

    def ln_apply(sl, xn_sb, kind, c_done, u_b):
        for dc in range(NDC):
            xn = xn_sb[:, dc * TSL : (dc + 1) * TSL]
            em.instr("vector", "tensor_sub", u_b[:], xc(dc, sl), mst_sb[:],
                     waits=[c_done], inc="dv")
            if kind == "attnb":
                em.instr("vector", "tensor_mul", u_b[:], u_b[:], cst_sb[:], inc="dv")
                em.instr("vector", "tensor_scalar_mul", xn, u_b[:],
                         lnw_sb[:, dc : dc + 1], inc="dv")
            else:
                em.instr("vector", "tensor_mul", xn, u_b[:], cst_sb[:], inc="dv")
        return em.last("dv")

    class WStream:

        def __init__(self, buf, nslots, blkcols, sems):
            self.buf = buf
            self.nslots = nslots
            self.blkcols = blkcols
            self.readers = [None] * nslots
            self.sems = sems
            self.i = 0

        def slot_ap(self, i):
            c = (i % self.nslots) * self.blkcols
            return self.buf[:, c : c + self.blkcols]

        def load(self, src_tiled, blk, nchunks):
            return self.load_ap(src_tiled[blk], nchunks)

        def load_ap(self, ap2d, nchunks):
            i = self.i
            self.i += 1
            sem = self.sems[i % self.nslots]
            em.instr("gpsimd", "dma_start", self.slot_ap(i), ap2d,
                     waits=[self.readers[i % self.nslots]], inc=sem, incn=16)
            return i, em.last(sem)

        def set_reader(self, i, marker):
            self.readers[i % self.nslots] = marker

    def emit_attn_a(next_kind):
        with ExitStack() as ph:
            P = ph.enter_context
            xn_sb = P(nc.sbuf_tensor("a_xn", [128, NDC * TSL], BF16))
            ctx_sb = P(nc.sbuf_tensor("a_ctx", [128, NDC * TSL], BF16))
            phiq_sb = P(nc.sbuf_tensor("a_phiq", [128, TSL], BF16))
            scrA = P(nc.sbuf_tensor("a_scrA", [128, TSL], F32))
            scrB = P(nc.sbuf_tensor("a_scrB", [128, TSL], F32))
            scrC = P(nc.sbuf_tensor("a_scrC", [128, TSL], F32))
            zeros_sb = P(nc.sbuf_tensor("a_zeros", [128, TSL], F32))
            carry_sb = P(nc.sbuf_tensor("a_carry", [128, NDC], F32))
            xsq_sb = P(nc.sbuf_tensor("a_xsq", [128, 2 * TSL], BF16))
            t_scr = P(nc.sbuf_tensor("a_st_t", [128, TSL], F32))
            u_scr = P(nc.sbuf_tensor("a_st_u", [128, TSL], BF16))
            wbuf = P(nc.sbuf_tensor("a_wbuf", [128, 6 * NDC * 128], BF16))
            psum = P(nc.psum_tensor("a_psum", [128, 8 * 512], F32))
            pool = PsumPool(psum, 8)
            ws = WStream(wbuf, 6, NDC * 128, ["w0", "w1", "w2", "w3", "w4", "w5"])

            em.instr("vector", "memset", zeros_sb[:], 0.0, waits=[consts_ready], inc="dv")
            em.instr("gpsimd", "memset", carry_sb[:], 0.0, inc="gp")
            carry_mark = [em.last("gp")] * NDC

            pending = {}

            def prefetch(sl, ec):
                for name, src in (("q", wqa), ("k", wka), ("v", wva)):
                    pending[(sl, ec, name)] = ws.load(src, ec, NDC)

            prefetch(0, 0)
            c_sl = ln_stats(0, pool, "plain", [x_sl0], xsq_sb, t_scr, None)
            c_next = None
            res_sl0 = None
            for sl in range(NSL):
                xn_ready = ln_apply(sl, xn_sb, "plain", c_sl, u_scr)
                if sl == 0:
                    c_sl = ln_stats(1, pool, "plain", [x_loaded], xsq_sb, t_scr,
                                    xn_ready)
                elif next_kind is not None:
                    c_next = ln_stats(0, pool, next_kind, [res_sl0], xsq_sb,
                                      t_scr, xn_ready)
                last_ctx = None
                for ec in range(NDC):
                    if ec + 1 < NDC:
                        prefetch(sl, ec + 1)
                    ps = {}
                    for name in ("q", "k", "v"):
                        i, loaded = pending.pop((sl, ec, name))
                        pp, wv, slot = pool.alloc(2)
                        for nh in range(2):
                            for dc in range(NDC):
                                mk = em.instr(
                                    "tensor", "matmul",
                                    pp[:, nh * 512 : nh * 512 + 512],
                                    ws.slot_ap(i)[:, dc * 128 : (dc + 1) * 128],
                                    xn_sb[:, dc * TSL + nh * 512 : dc * TSL + nh * 512 + 512],
                                    start=(dc == 0), stop=(dc == NDC - 1),
                                    waits=([loaded, xn_ready] + wv) if (nh == 0 and dc == 0) else (),
                                    inc="pe" if (nh == 1 and dc == NDC - 1) else None,
                                )
                        ws.set_reader(i, mk)
                        ps[name] = (pp, slot, mk)
                    ppq, slotq, mmq = ps["q"]
                    ppk, slotk, mmk = ps["k"]
                    ppv, slotv, mmv = ps["v"]
                    bq_ap = b_sb[:, ec : ec + 1]
                    bqs_ap = bqs_sb[:, ec : ec + 1]
                    bk_ap = b_sb[:, 16 + ec : 16 + ec + 1]
                    bv_ap = b_sb[:, 32 + ec : 32 + ec + 1]
                    em.instr("scalar", "activation", scrA[:], ppq[:, 0:TSL], AF.Relu,
                             scale=SCALE_A, bias=bqs_ap, waits=[mmq, em.last("dv")],
                             inc="ac")
                    rq_done = em.last("ac")
                    em.instr("vector", "tensor_scalar", scrB[:], ppq[:, 0:TSL],
                             bq_ap, 0.0, OP.add, OP.min, waits=[mmq, rq_done], inc="dv")
                    mnq_done = em.last("dv")
                    pool.add_reader(slotq, mnq_done)
                    em.instr("scalar", "activation", scrC[:], scrB[:], AF.Exp,
                             bias=LN_SCALE_A, waits=[mnq_done], inc="ac")
                    eq_done = em.last("ac")
                    em.instr("vector", "tensor_add", phiq_sb[:], scrA[:], scrC[:],
                             waits=[eq_done], inc="dv")
                    em.instr("scalar", "activation", scrA[:], ppk[:, 0:TSL], AF.Relu,
                             bias=bk_ap, waits=[mmk, em.last("dv")], inc="ac")
                    rk_done = em.last("ac")
                    em.instr("vector", "tensor_scalar", scrB[:], ppk[:, 0:TSL],
                             bk_ap, 0.0, OP.add, OP.min, waits=[mmk, rk_done], inc="dv")
                    mnk_done = em.last("dv")
                    pool.add_reader(slotk, mnk_done)
                    em.instr("scalar", "activation", scrC[:], scrB[:], AF.Exp,
                             waits=[mnk_done], inc="ac")
                    ek_done = em.last("ac")
                    em.instr("vector", "tensor_add", scrA[:], scrA[:], scrC[:],
                             waits=[ek_done], inc="dv")
                    em.instr("vector", "scalar_tensor_tensor", scrB[:], ppv[:, 0:TSL],
                             bv_ap, scrA[:], OP.add, OP.mult, waits=[mmv], inc="dv")
                    pool.add_reader(slotv, em.last("dv"))
                    em.instr("vector", "tensor_tensor_scan", scrC[:], scrB[:], zeros_sb[:],
                             carry_sb[:, ec : ec + 1], OP.add, OP.add,
                             waits=[carry_mark[ec]], inc="dv")
                    scan_done = em.last("dv")
                    if sl + 1 < NSL:
                        em.instr("scalar", "copy", carry_sb[:, ec : ec + 1],
                                 scrC[:, TSL - 1 : TSL], waits=[scan_done], inc="ac")
                        carry_mark[ec] = em.last("ac")
                    em.instr("vector", "tensor_mul",
                             ctx_sb[:, ec * TSL : (ec + 1) * TSL], phiq_sb[:], scrC[:],
                             inc="dv")
                    last_ctx = em.last("dv")
                wo_pending = {}
                wo_pending[0] = ws.load(woa, 0, NDC)
                wo_pending[1] = ws.load(woa, 1, NDC)
                for eo in range(NDC):
                    if eo + 2 < NDC:
                        wo_pending[eo + 2] = ws.load(woa, eo + 2, NDC)
                    elif eo == NDC - 2 and sl + 1 < NSL:
                        prefetch(sl + 1, 0)
                    i, loaded = wo_pending.pop(eo)
                    pp, wv, slot = pool.alloc(2)
                    for nh in range(2):
                        for ec in range(NDC):
                            mk = em.instr(
                                "tensor", "matmul",
                                pp[:, nh * 512 : nh * 512 + 512],
                                ws.slot_ap(i)[:, ec * 128 : (ec + 1) * 128],
                                ctx_sb[:, ec * TSL + nh * 512 : ec * TSL + nh * 512 + 512],
                                start=(ec == 0), stop=(ec == NDC - 1),
                                waits=([loaded, last_ctx] + wv) if (nh == 0 and ec == 0) else (),
                                inc="pe" if (nh == 1 and ec == NDC - 1) else None,
                            )
                    ws.set_reader(i, mk)
                    bo_ap = b_sb[:, 48 + eo : 48 + eo + 1]
                    em.instr("vector", "scalar_tensor_tensor", xc(eo, sl),
                             pp[:, 0:TSL], bo_ap, xc(eo, sl), OP.add, OP.add,
                             waits=[mk], inc="dv")
                    pool.add_reader(slot, em.last("dv"))
                if sl == 0:
                    res_sl0 = em.last("dv")
        return c_next, em.last("dv")

    def emit_ffn(c_sl, prev_res_sl1, final, tag, next_kind):
        with ExitStack() as ph:
            P = ph.enter_context
            psum = P(nc.psum_tensor(tag + "_psum", [128, 8 * 512], F32))
            pool = PsumPool(psum, 6)
            cs_ps = psum[:, 6 * 512 : 8 * 512]
            gu_done = None
            cs_reader = None
            stage_mk = [None, None]
            c_next = None
            res_sl0 = None
            for sl in range(NSL):
                with ExitStack() as slsc:
                    Q = slsc.enter_context
                    xnsc = ExitStack()
                    X_ = xnsc.enter_context
                    xn_sb = X_(nc.sbuf_tensor(f"{tag}{sl}_xn", [128, NDC * TSL], BF16))
                    with ExitStack() as lnsc:
                        L = lnsc.enter_context
                        t_scr = L(nc.sbuf_tensor(f"{tag}{sl}_st_t", [128, TSL], F32))
                        u_scr = L(nc.sbuf_tensor(f"{tag}{sl}_st_u", [128, TSL], BF16))
                        xsq_sb = L(nc.sbuf_tensor(f"{tag}{sl}_xsq", [128, 2 * TSL], BF16))
                        xn_ready = ln_apply(sl, xn_sb, "plain", c_sl, u_scr)
                        lnguard = xn_ready
                        if sl == 0:
                            c_sl = lnguard = ln_stats(1, pool, "plain",
                                                      [prev_res_sl1], xsq_sb,
                                                      t_scr, xn_ready)
                        elif next_kind is not None:
                            c_next = lnguard = ln_stats(0, pool, next_kind,
                                                        [res_sl0], xsq_sb,
                                                        t_scr, xn_ready)
                    act_sb = Q(nc.sbuf_tensor(f"{tag}{sl}_act", [128, NHC * TSL], BF16,
                                              side="right"))
                    with ExitStack() as gusc:
                        G = gusc.enter_context
                        gu_buf = G(nc.sbuf_tensor(f"{tag}{sl}_gu", [128, 6 * 8 * 128], BF16))
                        g_scr = G(nc.sbuf_tensor(f"{tag}{sl}_g", [128, 2 * TSL], BF16))
                        gws = WStream(gu_buf, 6, 8 * 128,
                                      ["w0", "w1", "w2", "w3", "w4", "w5"])
                        gws.readers = [lnguard] * 6
                        gu_jobs = []
                        for hc_ in range(NHC):
                            for src_ in (wg, wu):
                                for hf in range(2):
                                    gu_jobs.append(src_[hc_, hf])
                        gu_pend = {}
                        gu_next = [0]

                        def gu_load():
                            j = gu_next[0]
                            gu_next[0] += 1
                            gu_pend[j] = gws.load_ap(gu_jobs[j], 8)

                        for _ in range(6):
                            gu_load()
                        act_mk = {}
                        g_war = [None, None]

                        def emit_cs_act(hcl):
                            for nh in range(2):
                                em.instr("tensor", "matmul",
                                         cs_ps[:, nh * 512 : nh * 512 + 512],
                                         ones_sb[:, :],
                                         act_sb[:, hcl * TSL + nh * 512 :
                                                hcl * TSL + nh * 512 + 512],
                                         start=(hcl == 0), stop=(hcl == NHC - 1),
                                         waits=[act_mk[hcl], cs_reader]
                                         if hcl == 0 else [act_mk[hcl]],
                                         inc="pe" if (hcl == NHC - 1 and nh == 1) else None)

                        for hc in range(NHC):
                            pps = {}
                            for nmi, name in enumerate(("g", "u")):
                                j0 = (hc * 2 + nmi) * 2
                                i0, ld0 = gu_pend.pop(j0)
                                i1, ld1 = gu_pend.pop(j0 + 1)
                                pp, wv, slot = pool.alloc(2)
                                for nh in range(2):
                                    for dc in range(NDC):
                                        ii = i0 if dc < 8 else i1
                                        mk = em.instr(
                                            "tensor", "matmul",
                                            pp[:, nh * 512 : nh * 512 + 512],
                                            gws.slot_ap(ii)[:, (dc % 8) * 128 : (dc % 8 + 1) * 128],
                                            xn_sb[:, dc * TSL + nh * 512 : dc * TSL + nh * 512 + 512],
                                            start=(dc == 0), stop=(dc == NDC - 1),
                                            waits=([ld0, ld1, xn_ready] + wv)
                                            if (nh == 0 and dc == 0) else (),
                                            inc="pe" if (nh == 1 and dc == NDC - 1) else None,
                                        )
                                ws_mk = mk
                                gws.set_reader(i0, ws_mk)
                                gws.set_reader(i1, ws_mk)
                                pps[name] = (pp, slot, ws_mk)
                                for _ in range(2):
                                    if gu_next[0] < len(gu_jobs):
                                        gu_load()
                            ppg, slotg, mmg = pps["g"]
                            ppu, slotu, mmu = pps["u"]
                            gp = g_scr[:, (hc % 2) * TSL : (hc % 2 + 1) * TSL]
                            em.instr("scalar", "activation", gp, ppg[:, 0:TSL],
                                     AF.Silu, scale=sgu_sb[:, hc : hc + 1],
                                     waits=[mmg, g_war[hc % 2]], inc="ac")
                            g_done = em.last("ac")
                            pool.add_reader(slotg, g_done)
                            em.instr("vector", "scalar_tensor_tensor",
                                     act_sb[:, hc * TSL : (hc + 1) * TSL],
                                     gp, sgu_sb[:, NHC + hc : NHC + hc + 1],
                                     ppu[:, 0:TSL], OP.mult, OP.mult,
                                     waits=[mmu, g_done], inc="dv")
                            act_mk[hc] = g_war[hc % 2] = em.last("dv")
                            pool.add_reader(slotu, act_mk[hc])
                            if hc >= 2:
                                emit_cs_act(hc - 2)
                        for hcl in (NHC - 2, NHC - 1):
                            emit_cs_act(hcl)
                        gu_done = em.last("pe")
                    xnsc.close()
                    with ExitStack() as dnsc:
                        DN = dnsc.enter_context
                        dn_buf = DN(nc.sbuf_tensor(f"{tag}{sl}_dn", [128, 2 * NHC * 128], BF16))
                        ca_row = DN(nc.sbuf_tensor(f"{tag}{sl}_carow", [1, TSL], BF16))
                        if final:
                            stage = DN(nc.sbuf_tensor(f"{tag}{sl}_stage", [128, 2 * TSL], F32))
                        dws = WStream(dn_buf, 2, NHC * 128, ["w2", "w3"])
                        dws.readers = [gu_done, gu_done]
                        em.instr("vector", "tensor_copy", ca_row[0:1, :],
                                 cs_ps[0:1, 0:TSL], waits=[gu_done], inc="dv")
                        ca_done = em.last("dv")
                        cs_reader = ca_done
                        dn_pending = {0: dws.load(wd, 0, NHC)}
                        for dout in range(NDC):
                            if dout + 1 < NDC:
                                dn_pending[dout + 1] = dws.load(wd, dout + 1, NHC)
                            i, loaded = dn_pending.pop(dout)
                            pp, wv, slot = pool.alloc(2)
                            for nh in range(2):
                                for hc in range(NHC):
                                    em.instr(
                                        "tensor", "matmul",
                                        pp[:, nh * 512 : nh * 512 + 512],
                                        dws.slot_ap(i)[:, hc * 128 : (hc + 1) * 128],
                                        act_sb[:, hc * TSL + nh * 512 : hc * TSL + nh * 512 + 512],
                                        start=(hc == 0), stop=False,
                                        waits=([loaded] + wv) if (nh == 0 and hc == 0) else (),
                                    )
                                mk = em.instr(
                                    "tensor", "matmul",
                                    pp[:, nh * 512 : nh * 512 + 512],
                                    neg128_sb[0:1, :],
                                    ca_row[0:1, nh * 512 : nh * 512 + 512],
                                    start=False, stop=True,
                                    waits=[ca_done] if nh == 0 else (),
                                    inc="pe" if nh == 1 else None,
                                )
                            dws.set_reader(i, mk)
                            sd_ap = sdn_sb[:, dout : dout + 1]
                            if final:
                                sslot = stage[:, (dout % 2) * TSL : (dout % 2 + 1) * TSL]
                                em.instr("vector", "scalar_tensor_tensor", sslot,
                                         pp[:, 0:TSL], sd_ap, xc(dout, sl),
                                         OP.mult, OP.add,
                                         waits=[mk, stage_mk[dout % 2]], inc="dv")
                                pool.add_reader(slot, em.last("dv"))
                                osem = "oA" if dout % 2 == 0 else "oB"
                                em.instr("sync", "dma_start",
                                         y[dout, :, sl * TSL : (sl + 1) * TSL], sslot,
                                         waits=[em.last("dv")], inc=osem, incn=16)
                                stage_mk[dout % 2] = em.last(osem)
                            else:
                                em.instr("vector", "scalar_tensor_tensor",
                                         xc(dout, sl), pp[:, 0:TSL], sd_ap,
                                         xc(dout, sl), OP.mult, OP.add,
                                         waits=[mk], inc="dv")
                                pool.add_reader(slot, em.last("dv"))
                        if sl == 0:
                            res_sl0 = em.last("dv")
            if final:
                em.wait("sync", em.last("oA"), em.last("oB"))
            return c_next, em.last("dv")

    def emit_attn_b(c_sl, prev_res_sl1, next_kind):
        with ExitStack() as ph:
            P = ph.enter_context
            maskd_sb = P(nc.sbuf_tensor("b_maskd", [128, 128], BF16))
            maskw_sb = P(nc.sbuf_tensor("b_maskw", [128, 128], BF16))
            i32_sb = P(nc.sbuf_tensor("b_i32", [128, 128], F32))
            cos_sb = P(nc.sbuf_tensor("b_cos", [128, TSL], BF16))
            sin_sb = P(nc.sbuf_tensor("b_sin", [128, TSL], BF16))
            krot_sb = P(nc.sbuf_tensor("b_krot", [128, NKV * S], BF16))
            vtm_sb = P(nc.sbuf_tensor("b_vtm", [128, NKV * S], BF16))
            wsbuf = P(nc.sbuf_tensor("b_wsbuf", [128, 2 * NDC * 128], BF16))
            qrot_sb = P(nc.sbuf_tensor("b_qrot", [128, 4 * TSL], BF16))
            vbf_sb = P(nc.sbuf_tensor("b_vbf", [128, TSL], F32))
            u1_sb = P(nc.sbuf_tensor("b_u1", [128, TSL], BF16))
            u2_sb = P(nc.sbuf_tensor("b_u2", [128, TSL], BF16))
            u2s_sb = P(nc.sbuf_tensor("b_u2s", [128, TSL], BF16))
            attnT_sb = P(nc.sbuf_tensor("b_attnT", [128, 3 * 512], BF16))
            rden_sb = P(nc.sbuf_tensor("b_rden", [1, TSL], BF16))
            zrow_sb = P(nc.sbuf_tensor("b_zrow", [128, 512], BF16))
            bcast_sb = P(nc.sbuf_tensor("b_bcast", [128, TSL], BF16))
            arena = P(nc.psum_tensor("b_arena", [128, 8 * 512], F32))
            rotb = [arena[:, 0:512], arena[:, 512:1024]]
            proj_ps = arena[:, 2 * 512 : 4 * 512]
            av_ps = arena[:, 4 * 512 : 6 * 512]
            den_ps = arena[:, 6 * 512 : 8 * 512]

            em.instr("sync", "dma_start", maskd_sb[:], maskd_c, waits=[consts_ready],
                     inc="dC", incn=16)
            em.instr("sync", "dma_start", maskw_sb[:], maskw_c, inc="dC", incn=16)
            em.instr("sync", "dma_start", i32_sb[:], ident32_c, inc="dC", incn=16)
            masks_ok = em.last("dC")
            em.wait("vector", masks_ok)
            em.wait("tensor", masks_ok)

            em.instr("vector", "memset", zrow_sb[:], 0.0, inc="dv")
            zrow_ok = em.last("dv")
            ws = WStream(wsbuf, 2, NDC * 128, ["w0", "w1"])
            qr_mk = [None] * 4
            qr_war = [None] * 4
            kr_done = {}
            vtm_ok = {}
            vbf_war = None
            rope_done = None
            attnT_war = [None] * 3
            rot_war = [None, None]
            proj_war = None
            av_war = None
            den_war = None
            rden_war = None
            bcast_war = None
            ao_last = None
            c_next = None
            res_sl0 = None
            carry_readers = None

            for sl in range(NSL):
                qt_lo, qt_hi = sl * 8, sl * 8 + 7
                em.instr("sync", "dma_start", cos_sb[:],
                         cos2_c[:, sl * TSL : (sl + 1) * TSL],
                         waits=[rope_done or consts_ready], inc="dC", incn=16)
                em.instr("sync", "dma_start", sin_sb[:],
                         sin2_c[:, sl * TSL : (sl + 1) * TSL], inc="dC", incn=16)
                cs_ok = em.last("dC")
                jobs = []
                for kvh in range(NKV):
                    jobs.append((wkb, kvh))
                    jobs.append((wvb, kvh))
                for h in range(NH):
                    jobs.append((wqb, h))
                for eo in range(NDC):
                    jobs.append((wob, eo))
                pend = {}
                for j in range(2):
                    pend[j] = ws.load(jobs[j][0], jobs[j][1], NDC)
                nextload = [2]

                def consume(j):
                    return pend.pop(j)

                def after_use(i, mk):
                    ws.set_reader(i, mk)
                    if nextload[0] < len(jobs):
                        j = nextload[0]
                        nextload[0] += 1
                        pend[j] = ws.load(jobs[j][0], jobs[j][1], NDC)

                setup_pool = PsumPool(arena, 8)
                if carry_readers is not None:
                    setup_pool.readers = [list(r) for r in carry_readers]

                with ExitStack() as slsc:
                    Q = slsc.enter_context
                    xn_sb = Q(nc.sbuf_tensor(f"b{sl}_xn", [128, NDC * TSL], BF16))
                    lnsc = ExitStack()
                    L = lnsc.enter_context
                    t_scr = L(nc.sbuf_tensor(f"b{sl}_st_t", [128, TSL], F32))
                    u_scr = L(nc.sbuf_tensor(f"b{sl}_st_u", [128, TSL], BF16))
                    xsq_sb = L(nc.sbuf_tensor(f"b{sl}_xsq", [128, 2 * TSL], BF16))
                    xn_ready = ln_apply(sl, xn_sb, "attnb", c_sl, u_scr)

                    def proj_setup(jidx):
                        i, loaded = consume(jidx)
                        pp, wv, slot = setup_pool.alloc(2)
                        for nh in range(2):
                            for dc in range(NDC):
                                mk = em.instr(
                                    "tensor", "matmul",
                                    pp[:, nh * 512 : nh * 512 + 512],
                                    ws.slot_ap(i)[:, dc * 128 : (dc + 1) * 128],
                                    xn_sb[:, dc * TSL + nh * 512 : dc * TSL + nh * 512 + 512],
                                    start=(dc == 0), stop=(dc == NDC - 1),
                                    waits=([loaded, xn_ready] + wv)
                                    if (nh == 0 and dc == 0) else (),
                                    inc="pe" if (nh == 1 and dc == NDC - 1) else None,
                                )
                        return pp, slot, mk, i

                    def rope(pp, dst_ap, mm_mk, war=()):
                        em.instr("vector", "tensor_mul", u1_sb[:], pp[:, 0:TSL],
                                 cos_sb[:], waits=[mm_mk, cs_ok], inc="dv")
                        em.instr("vector", "tensor_mul", u2_sb[:], pp[:, 0:TSL],
                                 sin_sb[:], inc="dv")
                        em.instr("vector", "tensor_copy", u2s_sb[0:64, :],
                                 u2_sb[64:128, :], inc="dv")
                        em.instr("vector", "tensor_copy", u2s_sb[64:128, :],
                                 u2_sb[0:64, :], inc="dv")
                        em.instr("vector", "tensor_add", dst_ap[:, :],
                                 u1_sb[:], u2s_sb[:], waits=list(war), inc="dv")
                        return em.last("dv")

                    for kvh in range(NKV):
                        ppk, slotk, mmk, ik = proj_setup(2 * kvh)
                        kdst = krot_sb[:, kvh * S + sl * TSL : kvh * S + (sl + 1) * TSL]
                        kr = rope(ppk, kdst, mmk)
                        kr_done[(kvh, sl)] = kr
                        setup_pool.add_reader(slotk, kr)
                        after_use(ik, kr)
                        ppv, slotv, mmv, iv = proj_setup(2 * kvh + 1)
                        em.instr("scalar", "copy", vbf_sb[:], ppv[:, 0:TSL],
                                 waits=[mmv, vbf_war], inc="ac")
                        vb_ok = em.last("ac")
                        setup_pool.add_reader(slotv, vb_ok)
                        after_use(iv, vb_ok)
                        tps, twv, tslot = setup_pool.alloc(2)
                        for j in range(8):
                            tmk = em.instr("tensor", "transpose",
                                           tps[:, j * 128 : (j + 1) * 128],
                                           vbf_sb[:, j * 128 : (j + 1) * 128],
                                           i32_sb[:],
                                           waits=([vb_ok] + twv) if j == 0 else (),
                                           inc="pe" if j == 7 else None)
                        em.instr("scalar", "copy",
                                 vtm_sb[:, (kvh * NDC + sl * 8) * 128 :
                                        (kvh * NDC + sl * 8 + 8) * 128],
                                 tps[:, 0:1024], waits=[tmk], inc="ac")
                        vt_ok = em.last("ac")
                        setup_pool.add_reader(tslot, vt_ok)
                        vtm_ok[(kvh, sl)] = vt_ok
                        vbf_war = vt_ok

                    if sl == 0:
                        c_sl = ln_stats(1, setup_pool, "attnb", [prev_res_sl1],
                                        xsq_sb, t_scr, xn_ready)
                    elif next_kind is not None:
                        c_next = ln_stats(0, setup_pool, next_kind, [res_sl0],
                                          xsq_sb, t_scr, xn_ready)
                    lnsc.close()
                    attnout_sb = Q(nc.sbuf_tensor(f"b{sl}_ao", [128, NH * TSL], BF16,
                                                  side="right"))
                    sr = setup_pool.readers
                    rot_war = [sr[0][-1] if sr[0] else None,
                               sr[1][-1] if sr[1] else None]
                    proj_war = sr[3][-1] if sr[3] else (sr[2][-1] if sr[2] else None)
                    pw2 = sr[2][-1] if sr[2] else None
                    av_war = sr[5][-1] if sr[5] else (sr[4][-1] if sr[4] else None)
                    aw2 = sr[4][-1] if sr[4] else None
                    den_war = sr[7][-1] if sr[7] else (sr[6][-1] if sr[6] else None)
                    dw2 = sr[6][-1] if sr[6] else None

                    qpend = {}
                    rot_i = [0]
                    buf_i = [0]

                    def qproj_start(h):
                        qpend[h] = consume(8 + h)

                    def qproj_gen(h, extra=()):
                        i, loaded = qpend[h]
                        for part in range(4):
                            nh, half = part // 2, part % 2
                            for dc in range(half * 8, half * 8 + 8):
                                mk = em.instr(
                                    "tensor", "matmul",
                                    proj_ps[:, nh * 512 : nh * 512 + 512],
                                    ws.slot_ap(i)[:, dc * 128 : (dc + 1) * 128],
                                    xn_sb[:, dc * TSL + nh * 512 : dc * TSL + nh * 512 + 512],
                                    start=(dc == 0), stop=(dc == NDC - 1),
                                    waits=([loaded, xn_ready, proj_war] + list(extra))
                                    if (part == 0 and dc == 0) else (),
                                    inc="pe" if part == 3 and dc == NDC - 1 else None,
                                )
                                if not (part == 3 and dc == NDC - 1):
                                    yield None
                        qs = h % 4
                        qr_mk[qs] = rope(proj_ps, qrot_sb[:, qs * TSL : (qs + 1) * TSL],
                                         mk, war=[qr_war[qs]])
                        after_use(i, qr_mk[qs])
                        yield qr_mk[qs]

                    def qproj_part(h, part, extra=()):
                        g = qproj_gen(h, extra)
                        r = None
                        for r in g:
                            pass
                        return r

                    qproj_start(0)
                    proj_war = qproj_part(0, 0, extra=[pw2] if pw2 else [])
                    qproj_start(1)
                    proj_war = qproj_part(1, 0)

                    pending_finish = [None]
                    for h in range(NH):
                        kvh = h // 4
                        qs = h % 4
                        chunks = []
                        for kt in range(max(0, qt_lo - 4), qt_hi + 1):
                            qlo_t = max(kt, qt_lo)
                            qhi_t = min(kt + 4, qt_hi)
                            if qhi_t - qlo_t + 1 == 5:
                                chunks.append((kt, qlo_t, qlo_t + 3))
                                chunks.append((kt, qhi_t, qhi_t))
                            else:
                                chunks.append((kt, qlo_t, qhi_t))
                        nxt = h + 2
                        proj_g = None
                        per_chunk = 0
                        if nxt < NH:
                            qproj_start(nxt)
                            proj_g = qproj_gen(nxt)
                            per_chunk = max(1, (33 + len(chunks) - 2) // max(1, len(chunks) - 1))
                        pending = []
                        den_mk = av_mk = None

                        def emit_denav(pc):
                            nonlocal den_mk, av_mk
                            kt, qlo, qhi, rb, buf, t_waits = pc
                            sl_kt = 0 if kt < 8 else 1
                            strip = attnT_sb[:, buf * 512 : buf * 512 + (qhi - qlo + 1) * 128]
                            groups = []
                            for qtg in range(qlo, qhi + 1):
                                bank = 0 if qtg < qt_lo + 4 else 1
                                if groups and groups[-1][2] == bank:
                                    groups[-1][1] = qtg
                                else:
                                    groups.append([qtg, qtg, bank])
                            for g_lo, g_hi, g_bank in groups:
                                sp = (g_bank == 0 and kt == qt_lo + 3) or \
                                     (g_bank == 1 and kt == qt_hi)
                                oc = slice((g_lo - qt_lo) * 128, (g_hi + 1 - qt_lo) * 128)
                                mc = slice((g_lo - qlo) * 128, (g_hi + 1 - qlo) * 128)
                                den_mk = em.instr(
                                    "tensor", "matmul", den_ps[0:1, oc],
                                    ones_sb[:, 0:1], strip[:, mc],
                                    start=False, stop=sp, waits=list(t_waits),
                                    inc="pe")
                                av_mk = em.instr(
                                    "tensor", "matmul", av_ps[:, oc],
                                    vtm_sb[:, (kvh * NDC + kt) * 128 :
                                           (kvh * NDC + kt + 1) * 128],
                                    strip[:, mc],
                                    start=False, stop=sp,
                                    waits=[vtm_ok[(kvh, sl_kt)]], inc="pe")
                            attnT_war[buf] = av_mk

                        for ci, (kt, qlo, qhi) in enumerate(chunks):
                            nqc = qhi - qlo + 1
                            sl_kt = 0 if kt < 8 else 1
                            rb = rot_i[0] % 2
                            rot_i[0] += 1
                            buf_i[0] += 1
                            buf = buf_i[0] % 3
                            smk = em.instr(
                                "tensor", "matmul", rotb[rb][:, 0 : nqc * 128],
                                krot_sb[:, kvh * S + kt * 128 : kvh * S + (kt + 1) * 128],
                                qrot_sb[:, qs * TSL + (qlo - qt_lo) * 128 :
                                        qs * TSL + (qhi + 1 - qt_lo) * 128],
                                start=True, stop=True,
                                waits=[kr_done[(kvh, sl_kt)], qr_mk[qs], rot_war[rb]],
                                inc="pe")
                            qr_war[qs] = smk
                            strip = attnT_sb[:, buf * 512 : buf * 512 + nqc * 128]
                            em.instr("scalar", "activation", strip,
                                     rotb[rb][:, 0 : nqc * 128], AF.Exp,
                                     scale=SCALE_A, bias=-25.0,
                                     waits=[smk, attnT_war[buf]], inc="ac")
                            e_mk = em.last("ac")
                            rot_war[rb] = e_mk
                            t_waits = [e_mk]
                            if qlo == kt:
                                em.instr("vector", "tensor_mul",
                                         attnT_sb[:, buf * 512 : buf * 512 + 128],
                                         attnT_sb[:, buf * 512 : buf * 512 + 128],
                                         maskd_sb[:], waits=[e_mk], inc="dv")
                                t_waits.append(em.last("dv"))
                            if qhi == kt + 4:
                                em.instr("vector", "tensor_mul",
                                         attnT_sb[:, buf * 512 + (nqc - 1) * 128 :
                                                  buf * 512 + nqc * 128],
                                         attnT_sb[:, buf * 512 + (nqc - 1) * 128 :
                                                  buf * 512 + nqc * 128],
                                         maskw_sb[:], waits=[e_mk], inc="dv")
                                t_waits.append(em.last("dv"))
                            if ci == 0:
                                if pending_finish[0] is not None:
                                    pending_finish[0]()
                                    pending_finish[0] = None
                                ow = [zrow_ok, den_war, dw2]
                                for bk in range(2):
                                    em.instr("tensor", "matmul",
                                             den_ps[0:1, bk * 512 : bk * 512 + 512],
                                             ones_sb[0:1, 0:1], zrow_sb[0:1, :],
                                             start=True, stop=False,
                                             waits=ow if bk == 0 else (), inc="pe")
                                ow = [av_war, aw2]
                                for bk in range(2):
                                    em.instr("tensor", "matmul",
                                             av_ps[:, bk * 512 : bk * 512 + 512],
                                             ones_sb[:, :], zrow_sb[:, :],
                                             start=True, stop=False,
                                             waits=ow if bk == 0 else (), inc="pe")
                                dw2 = aw2 = None
                            if proj_g is not None and ci > 0:
                                for _ in range(per_chunk):
                                    mkq = next(proj_g, StopIteration)
                                    if mkq is StopIteration:
                                        proj_g = None
                                        break
                                    if mkq is not None:
                                        proj_war = mkq
                            if len(pending) >= 1:
                                emit_denav(pending.pop(0))
                            pending.append((kt, qlo, qhi, rb, buf, t_waits))
                        while pending:
                            emit_denav(pending.pop(0))
                        if proj_g is not None:
                            for mkq in proj_g:
                                if mkq is not None:
                                    proj_war = mkq
                            proj_g = None
                        em.instr("vector", "reciprocal", rden_sb[0:1, :],
                                 den_ps[0:1, 0:TSL],
                                 waits=[den_mk, rden_war], inc="dv", lowp=True)
                        rc_mk = em.last("dv")
                        den_war = rc_mk

                        def finish(h=h, av_mk=av_mk, rc_mk=rc_mk):
                            nonlocal rden_war, bcast_war, av_war, ao_last
                            bmks = []
                            for nh2 in range(2):
                                rb = rot_i[0] % 2
                                rot_i[0] += 1
                                bmk = em.instr(
                                    "tensor", "matmul", rotb[rb][:, 0:512],
                                    ones_sb[0:1, :],
                                    rden_sb[0:1, nh2 * 512 : nh2 * 512 + 512],
                                    start=True, stop=True,
                                    waits=[rc_mk, rot_war[rb]],
                                    inc="pe")
                                rden_war = bmk
                                em.instr("vector", "tensor_copy",
                                         bcast_sb[:, nh2 * 512 : nh2 * 512 + 512],
                                         rotb[rb][:, 0:512],
                                         waits=[bmk, bcast_war], inc="dv")
                                rot_war[rb] = em.last("dv")
                                bmks.append(em.last("dv"))
                            em.instr("vector", "tensor_mul",
                                     attnout_sb[:, h * TSL : (h + 1) * TSL],
                                     av_ps[:, 0:TSL], bcast_sb[:],
                                     waits=[av_mk] + bmks, inc="dv")
                            ao_last = av_war = bcast_war = em.last("dv")

                        pending_finish[0] = finish
                    pending_finish[0]()
                    pending_finish[0] = None

                    wo_pool = PsumPool(arena, 4)
                    wo_pool.readers = [[rot_war[0]], [rot_war[1]],
                                       [proj_war], [proj_war]]
                    for eo in range(NDC):
                        i, loaded = consume(24 + eo)
                        pp, wv, slot = wo_pool.alloc(2)
                        for nh in range(2):
                            for ei in range(NH):
                                mk = em.instr(
                                    "tensor", "matmul",
                                    pp[:, nh * 512 : nh * 512 + 512],
                                    ws.slot_ap(i)[:, ei * 128 : (ei + 1) * 128],
                                    attnout_sb[:, ei * TSL + nh * 512 :
                                               ei * TSL + nh * 512 + 512],
                                    start=(ei == 0), stop=(ei == NH - 1),
                                    waits=([loaded, ao_last] + wv)
                                    if (nh == 0 and ei == 0) else (),
                                    inc="pe" if (nh == 1 and ei == NH - 1) else None,
                                )
                        after_use(i, mk)
                        em.instr("vector", "tensor_add", xc(eo, sl), xc(eo, sl),
                                 pp[:, 0:TSL], waits=[mk], inc="dv")
                        wo_pool.add_reader(slot, em.last("dv"))
                    if sl == 0:
                        res_sl0 = em.last("dv")
                    rope_done = em.last("dv")
                    wr = wo_pool.readers
                    carry_readers = [wr[0], wr[1], wr[2], wr[3],
                                     [av_war], [av_war], [den_war], [den_war]]
        return c_next, em.last("dv")

    def emit_flush(x_ready):
        with ExitStack() as ph:
            P = ph.enter_context
            stage = P(nc.sbuf_tensor("fl_stage", [128, 2 * TSL], F32))
            dma_mk = [None] * 2
            k = 0
            for dc in range(NDC):
                for sl in range(NSL):
                    sslot = stage[:, (k % 2) * TSL : (k % 2 + 1) * TSL]
                    em.instr("vector", "tensor_copy", sslot, xc(dc, sl),
                             waits=[x_ready, dma_mk[k % 2]], inc="dv")
                    osem = "oA" if k % 2 == 0 else "oB"
                    em.instr("sync", "dma_start",
                             y[dc, :, sl * TSL : (sl + 1) * TSL], sslot,
                             waits=[em.last("dv")], inc=osem, incn=16)
                    dma_mk[k % 2] = em.last(osem)
                    k += 1
            em.wait("sync", em.last("oA"), em.last("oB"))

    def phase_gate():
        snap = [em.last(s) for s in ("pe", "dv", "ac")]
        for e in ("gpsimd", "sync", "tensor", "vector", "scalar"):
            em.wait(e, *snap)

    c, res1 = emit_attn_a("plain" if n_phases >= 2 else None)
    if n_phases >= 2:
        phase_gate()
        c, res1 = emit_ffn(c, res1, final=False, tag="f1",
                           next_kind="attnb" if n_phases >= 3 else None)
    if n_phases >= 3:
        phase_gate()
        c, res1 = emit_attn_b(c, res1,
                              next_kind="plain" if n_phases >= 4 else None)
    if n_phases >= 4:
        phase_gate()
        emit_ffn(c, res1, final=True, tag="f2", next_kind=None)
    else:
        phase_gate()
        emit_flush(res1)

    with nc.Block() as block:
        @block.sync
        def _(e):
            for fn in em.ops["sync"]:
                fn(e)

        @block.gpsimd
        def _(e):
            for fn in em.ops["gpsimd"]:
                fn(e)

        @block.tensor
        def _(e):
            for fn in em.ops["tensor"]:
                fn(e)

        @block.vector
        def _(e):
            for fn in em.ops["vector"]:
                fn(e)

        @block.scalar
        def _(e):
            for fn in em.ops["scalar"]:
                fn(e)

    st.close()
    return nc


def _prep_shared(ins):
    f32 = np.float32
    out = {}
    t44 = lambda a, r, c: np.ascontiguousarray(
        a.reshape(r, 128, c, 128).transpose(2, 0, 1, 3))
    t44c = lambda a, r, c: np.ascontiguousarray(
        a.reshape(r, 128, c, 128).transpose(2, 1, 0, 3).reshape(c, 128, r * 128))
    out["wqa"] = t44c(np.ascontiguousarray(ins["L0_attn_a_Wq"].T), NDC, NDC)
    out["wka"] = t44c(np.ascontiguousarray(ins["L0_attn_a_Wk"].T), NDC, NDC)
    out["wva"] = t44c(np.ascontiguousarray(ins["L0_attn_a_Wv"].T), NDC, NDC)
    out["woa"] = t44c(np.ascontiguousarray(ins["L0_attn_a_Wo"].T), NDC, NDC)
    cols = lambda v, n: np.ascontiguousarray(v.reshape(n, 128).T, dtype=f32)
    out["biases"] = np.concatenate(
        [cols(ins["L0_attn_a_b" + k], 16) for k in "qkvo"], axis=1)
    gu_pack = lambda a: np.ascontiguousarray(
        t44(a, NDC, NHC).astype(np.uint8).reshape(NHC, 2, 8, 128, 128)
        .transpose(0, 1, 3, 2, 4).reshape(NHC, 2, 128, 8 * 128))
    out["wg"] = gu_pack(np.ascontiguousarray(ins["L0_ffn_wq_gate"].T))
    out["wu"] = gu_pack(np.ascontiguousarray(ins["L0_ffn_wq_up"].T))
    out["wd"] = t44c(np.ascontiguousarray(ins["L0_ffn_wq_down"].T), NHC, NDC).astype(np.uint8)
    out["sgu"] = np.concatenate(
        [cols(ins["L0_ffn_s_gate"], NHC), cols(ins["L0_ffn_s_up"], NHC)], axis=1)
    out["sdn"] = cols(ins["L0_ffn_s_down"], NDC)
    perm = np.concatenate([np.arange(0, 128, 2), np.arange(1, 128, 2)])
    qp = ins["L0_attn_b_wq"].reshape(D, NH, 128)[:, :, perm]
    kp = ins["L0_attn_b_wk"].reshape(D, NKV, 128)[:, :, perm]
    vp = ins["L0_attn_b_wv"].reshape(D, NKV, 128)
    bpack = lambda t, nh_: np.ascontiguousarray(
        t.transpose(1, 0, 2).reshape(nh_, NDC, 128, 128)
        .transpose(0, 2, 1, 3).reshape(nh_, 128, NDC * 128))
    out["wqb"] = bpack(qp, NH)
    out["wkb"] = bpack(kp, NKV)
    out["wvb"] = bpack(vp, NKV)
    out["wob"] = t44c(ins["L0_attn_b_wo"], NDC, NH)
    out["lnw"] = cols(ins["L0_attn_b_ln_w"], NDC)
    out["ones_c"] = np.ones((128, 128), dtype=ml_dtypes.bfloat16)
    out["ident_c"] = np.eye(128, dtype=ml_dtypes.bfloat16)
    out["neg128_c"] = np.full((1, 128), -128.0, dtype=ml_dtypes.bfloat16)
    kk = np.arange(128)[:, None]
    qq = np.arange(128)[None, :]
    out["maskd_c"] = (kk <= qq).astype(ml_dtypes.bfloat16)
    out["maskw_c"] = (kk >= qq).astype(ml_dtypes.bfloat16)
    out["ident32_c"] = np.eye(128, dtype=f32)
    invf = 1.0 / 10000.0 ** (np.arange(0, HD, 2, dtype=np.float64) / HD)
    ang = np.arange(S, dtype=np.float64)[:, None] * invf[None, :]
    cosT = np.cos(ang).T.astype(ml_dtypes.bfloat16)
    sinT = np.sin(ang).T.astype(ml_dtypes.bfloat16)
    out["cos2_c"] = np.ascontiguousarray(np.concatenate([cosT, cosT], axis=0))
    out["sin2_c"] = np.ascontiguousarray(np.concatenate([sinT, -sinT], axis=0))
    return out


def run(inputs, n_phases=4, trace=False):
    shared = _prep_shared(inputs)
    x = np.asarray(inputs["L0_attn_a_x"], dtype=np.float32)
    B = x.shape[0]
    in_maps = []
    for b in range(B):
        m = dict(shared)
        m["xT"] = np.ascontiguousarray(x[b].T).reshape(NDC, 128, S)
        in_maps.append(m)
    nc = build_program(n_phases=n_phases)
    res = run_bass_kernel_spmd(nc, in_maps, core_ids=list(range(8)), trace=trace)
    outs = []
    for b in range(B):
        yb = res.results[b]["y"]
        outs.append(yb.reshape(D, S).T)
    return np.stack(outs, axis=0), res


def kernel(**inputs):
    out, _ = run(inputs, n_phases=4)
    return out.astype(np.float32)
